# version 08 (final)
# baseline (speedup 1.0000x reference)
import numpy as np
import ml_dtypes
import concourse.bass as bass
import concourse.mybir as mybir
from concourse.bass_utils import run_bass_kernel_spmd

BF = ml_dtypes.bfloat16



ENGINES = ("PE", "ACT", "DVE", "POOL", "SP")
DMAQ = ("SP", "ACT", "POOL")
NLANE = 8


class Sched:
    def __init__(self, nc):
        self.nc = nc
        self.prog = {e: [] for e in ENGINES}
        self.cnt = {e: 0 for e in ENGINES}
        self.dmacnt = {q: 0 for q in DMAQ}
        self.cccnt = 0
        self.maxw = {e: {} for e in ENGINES}
        self.tokview = {}
        self.state = {}
        self.esem = {e: nc.alloc_semaphore(f"sem_{e}") for e in ENGINES}
        self.dsem = {(q, l): nc.alloc_semaphore(f"dma_{q}{l}")
                     for q in DMAQ for l in range(NLANE)}
        self.dlane = {q: 0 for q in DMAQ}
        self.dcnt = {(q, l): 0 for q in DMAQ for l in range(NLANE)}
        self.csem = nc.alloc_semaphore("sem_cc")
        self.nsteps = 0

    def _semkey(self, tok):
        if tok[0] == "eng":
            return "eng:" + tok[1]
        if tok[0] == "dma":
            return "dma:%s%d" % tok[1]
        return "cc:"

    def _sem(self, tok):
        if tok[0] == "eng":
            return self.esem[tok[1]]
        if tok[0] == "dma":
            return self.dsem[tok[1]]
        return self.csem

    def _tokval(self, tok):
        return tok[-1]

    def _merge(self, eng, snap):
        mw = self.maxw[eng]
        for k, v in snap.items():
            if mw.get(k, 0) < v:
                mw[k] = v

    def _wait(self, eng, tok):
        if tok is None:
            return
        key = self._semkey(tok)
        v = self._tokval(tok)
        snap = self.tokview.get((key, v))
        if self.maxw[eng].get(key, 0) >= v:
            if snap:
                self._merge(eng, snap)
            return
        if snap:
            self._merge(eng, snap)
        self.maxw[eng][key] = v
        sem = self._sem(tok)
        self.prog[eng].append(("wait", sem, v))

    def _deps_for(self, reads, writes):
        deps = []
        for rk in reads:
            st = self.state.get(rk)
            if st and st["w"] is not None:
                deps.append(st["w"])
        for wk in writes:
            st = self.state.get(wk)
            if st:
                if st["w"] is not None:
                    deps.append(st["w"])
                deps.extend(st["r"])
        return deps

    def _update_state(self, reads, writes, tok):
        for wk in writes:
            self.state[wk] = {"w": tok, "r": []}
        for rk in reads:
            st = self.state.setdefault(rk, {"w": None, "r": []})
            st["r"].append(tok)

    def step(self, eng, emit, reads=(), writes=(), extra_deps=(), n_inc=1):
        assert eng in ENGINES
        for d in self._deps_for(list(reads), list(writes)) + list(extra_deps):
            self._wait(eng, d)
        self.cnt[eng] += n_inc
        tok = ("eng", eng, self.cnt[eng])
        snap = dict(self.maxw[eng])
        snap["eng:" + eng] = self.cnt[eng]
        self.tokview[("eng:" + eng, self.cnt[eng])] = snap
        self.prog[eng].append(("emit", emit, self.esem[eng], n_inc))
        self._update_state(reads, writes, tok)
        self.nsteps += 1
        return tok

    def dma(self, q, pairs, reads=(), writes=(), extra_deps=()):
        assert q in DMAQ
        lane = self.dlane[q]
        self.dlane[q] = (lane + 1) % NLANE
        prev = self.dcnt[(q, lane)]
        if prev:
            self._wait(q, ("dma", (q, lane), prev))
        for d in self._deps_for(list(reads), list(writes)) + list(extra_deps):
            self._wait(q, d)
        n = len(pairs)
        self.dcnt[(q, lane)] += 16 * n
        tok = ("dma", (q, lane), self.dcnt[(q, lane)])
        self.tokview[(self._semkey(tok), self.dcnt[(q, lane)])] = dict(self.maxw[q])
        self.prog[q].append(("dma", list(pairs), self.dsem[(q, lane)]))
        self._update_state(reads, writes, tok)
        self.nsteps += 1
        return tok

    def cc(self, emit, reads=(), writes=(), extra_deps=()):
        if self.cccnt:
            self._wait("POOL", ("cc", self.cccnt))
        for d in self._deps_for(list(reads), list(writes)) + list(extra_deps):
            self._wait("POOL", d)
        self.cccnt += 1
        tok = ("cc", self.cccnt)
        self.tokview[("cc:", self.cccnt)] = dict(self.maxw["POOL"])
        self.prog["POOL"].append(("emit", emit, self.csem, 1))
        self._update_state(reads, writes, tok)
        self.nsteps += 1
        return tok

    def alias(self, new_keys, old_keys, drop_old=True):
        toks = []
        for ok in old_keys:
            st = self.state.get(ok)
            if st:
                if st["w"] is not None:
                    toks.append(st["w"])
                toks.extend(st["r"])
                if drop_old:
                    del self.state[ok]
        for nk in new_keys:
            st = self.state.setdefault(nk, {"w": None, "r": []})
            st["r"] = st["r"] + toks

    def barrier_wait(self, eng, toks):
        for t in toks:
            self._wait(eng, t)

    def build(self):
        nc = self.nc
        with nc.Block() as blk:
            decs = {"SP": blk.sync, "PE": blk.tensor, "DVE": blk.vector,
                    "ACT": blk.scalar, "POOL": blk.gpsimd}
            for eng in ENGINES:
                items = self.prog[eng]

                def body(e, items=items):
                    for it in items:
                        if it[0] == "wait":
                            e.wait_ge(it[1], it[2])
                        elif it[0] == "emit":
                            inst = it[1](e)
                            assert inst is not None, "emit must return last instruction"
                            inst.then_inc(it[2], it[3])
                        elif it[0] == "dma":
                            for (out, in_, kw) in it[1]:
                                e.dma_start(out=out, in_=in_, **kw).then_inc(it[2], 16)
                decs[eng](body)




F32 = mybir.dt.float32
BF16 = mybir.dt.bfloat16
AF = mybir.ActivationFunctionType
ALU = mybir.AluOpType
AX = mybir.AxisListType

D = 2048
NB = 8
TOK = NB * 128
EPS = 1e-5
ISQ = 0.08838834764831845
RG = [[0, 1], [2, 3], [4, 5], [6, 7]]
MISC_PREFS = ("N1", "Xn", "oj", "KH", "VH", "gam", "cs", "kr32", "HE", "HT",
              "rold", "mk", "mu", "mc", "mv", "bzt", "zero", "hsc", "hsc2",
              "agc", "ags", "chc", "chox", "hint", "hin", "rowf", "xn2", "dum")
QTA_PREFS = ("QH", "prb", "prt", "stg", "ostg", "kvdT", "qdT", "SG", "cT", "Xn")


def ts(i, n=128):
    return slice(i * n, (i + 1) * n)


class MK:
    def __init__(self, nc, stop_after="x4"):
        self.nc = nc
        self.s = Sched(nc)
        self.stop = stop_after
        self.uid = 0
        self.rot3 = 0
        self.decl_io()
        self.alloc()

    def u(self):
        self.uid += 1
        return self.uid

    def decl_io(self):
        nc = self.nc
        self.inp = {}

        def di(name, shape, dt=F32):
            t = nc.dram_tensor(name, list(shape), dt, kind="ExternalInput")
            self.inp[name] = t
            return t

        di("x0", (TOK, D))
        di("aWq", (D, D)); di("aWk", (D, D)); di("aWv", (D, D)); di("aWo", (D, D))
        di("abq", (128, 16)); di("abk", (128, 16))
        di("abv", (1, D)); di("abo", (1, D))
        di("aln_g", (1, D)); di("aln_b", (1, D))
        di("fln1", (1, D)); di("fln2", (1, D))
        di("fkvd", (D, 512)); di("fqd", (D, 512))
        di("fku", (512, 1024)); di("fvu", (512, D)); di("fqu", (512, 1024))
        di("frk", (D, 1024)); di("frq", (512, 1024))
        di("fow", (D, D))
        di("fshg", (D, 1024)); di("fshu", (D, 1024)); di("fshd", (1024, D))
        di("frg", (7, D, 1024)); di("fru", (7, D, 1024)); di("frd", (7, 1024, D))
        di("frw", (D, 7)); di("frb", (1, 7))
        di("bWzT", (D, D)); di("bWnT", (D, D))
        di("bbz", (1, D)); di("bbn", (1, D))
        di("ident", (128, 128), BF16)
        di("mask1", (128, 128), BF16)
        di("mask2", (128, 128), BF16)
        di("ropeC", (TOK, 1024))
        di("ropeS", (TOK, 1024))
        di("par", (128, 1))
        di("identf", (128, 128))

        self.out = nc.dram_tensor("y", [TOK, D], F32, kind="ExternalOutput")

        nd = nc.dram_tensor
        self.agin_K = nd("agin_K", [2048, 1024], BF16, kind="Internal")
        self.agout_K = nd("agout_K", [4096, 1024], BF16, kind="Internal")
        self.agin_V = nd("agin_V", [1024, 2048], BF16, kind="Internal")
        self.agout_V = nd("agout_V", [2048, 2048], BF16, kind="Internal")
        self.agin_S = nd("agin_S", [128, 128], F32, kind="Internal")
        self.agout_S = nd("agout_S", [256, 128], F32, kind="Internal")
        self.qtd = nd("qtd", [2048, 1024], BF16, kind="Internal")
        self.otd = nd("otd", [2048, 1024], BF16, kind="Internal")
        self.rsp = nd("rsp", [TOK, D], F32, kind="Internal")

    def alloc(self):
        A = self.nc.alloc_sbuf_tensor
        self.STREAM = A("STREAM", [128, NB * D], F32)
        self.XT = A("XT", [128, 16 * 1024], BF16)
        self.QTA = A("QTA", [128, 16 * 1024], BF16)
        self.WA = A("WA", [128, 8 * 1024], BF16)
        self.WB = A("WB", [128, 8 * 1024], BF16)
        self.MISC = A("MISC", [128, 16 * 1024], BF16)
        self.PRB = self.QTA[:, 2048:8192]
        self.XNS = self.QTA[:, 2048:6144]
        self.PRT = self.QTA[:, 8192:12288]
        self.STG = self.QTA[:, 12288:14336]
        self.OSTG = self.QTA[:, 14336:16384]
        self.SCRa = self.MISC[:, 0:4096].bitcast(F32)
        self.SCRb = self.MISC[:, 4096:6144]
        self.SCRb2 = self.MISC[:, 14336:16384]
        self.KH = self.MISC[:, 0:4096]
        self.VH = self.MISC[:, 4096:8192]
        self.GAM = self.MISC[:, 8192:12288]
        self.CS = self.MISC[:, 12288:16384].bitcast(F32)
        self.MV = A("MV", [128, 4 * 24 + 12], F32)
        self.SM1 = A("SM1", [128, 8], F32)
        self.NM = A("NM", [128, 6], F32)
        self.SUMS = A("SUMS", [128, 6], F32)
        self.RCP = A("RCP", [128, 3], F32)
        self.DG = A("DG", [128, 3 * 128], BF16)
        self.IDENT = A("IDENT", [128, 128], BF16)
        self.MK1 = A("MK1", [128, 128], BF16)
        self.MK2 = A("MK2", [128, 128], BF16)
        self.BIASC = A("BIASC", [128, 32], F32)
        self.BROWB = A("BROWB", [1, D + 16], BF16)
        self.ROWB = A("ROWB", [1, D], BF16)
        self.PAR = A("PAR", [128, 2], F32)
        self.RT = A("RT", [128, 64], F32)
        self.WS = A("WS", [128, 64], F32)
        self.FRW = A("FRW", [128, 112], BF16)
        self.IDF = A("IDF", [128, 128], F32)
        self.ROWF = self.MISC.bitcast(F32)[0:1, 3968:4992]
        self.ONE1F = A("ONE1F", [1, 128], F32)
        self.ONE1 = A("ONE1", [1, 128], BF16)
        P = self.nc.alloc_psum_tensor
        self.PS = [P("PSA", [128, 1024], F32), P("PSB", [128, 1024], F32),
                   P("PSC", [128, 1024], F32), P("PSD", [128, 1024], F32)]

    def st(self, eng, emit, r=(), w=(), x=()):
        return self.s.step(eng, emit, reads=r, writes=w, extra_deps=x)

    def heat(self):
        def emit(e):
            return e.matmul(self.PS[2][:, 768:896], self.ONE1[:], self.ONE1[:],
                            start=True, stop=True)
        self.st("PE", emit, r=(("one1", 0),), w=(("heat", 0), ("ps", 2, 1)))

    def dma(self, q, pairs, r=(), w=(), x=()):
        return self.s.dma(q, [(a, b, k) for (a, b, k) in pairs], reads=r, writes=w,
                          extra_deps=x)

    def load_whalf(self, wsrc_ap, rows, c0, cw, slot, wkey, x=()):
        nch = rows // 128
        dst = slot[:, 0:nch * cw].rearrange("p (c k) -> p c k", k=cw)
        src = wsrc_ap[:, c0:c0 + cw].rearrange("(c p) k -> p c k", p=128)
        return self.dma("POOL", [(dst, src, {})], w=(wkey,), x=x)

    def phase_init(self):
        self.dma("SP", [(self.STREAM[:, j * D:(j + 1) * D],
                         self.inp["x0"].ap()[ts(j), :], {}) for j in range(NB)],
                 w=[("R", j) for j in range(NB)])
        self.dma("SP", [(self.IDENT[:], self.inp["ident"].ap(), {}),
                        (self.MK1[:], self.inp["mask1"].ap(), {}),
                        (self.MK2[:], self.inp["mask2"].ap(), {}),
                        (self.PAR[:, 0:1], self.inp["par"].ap(), {})],
                 w=(("consts", 0),))
        self.dma("SP", [(self.BIASC[:, 0:16], self.inp["abq"].ap(), {}),
                        (self.BIASC[:, 16:32], self.inp["abk"].ap(), {})],
                 w=(("biasc", 0),))
        self.dma("POOL", [(self.BROWB[0:1, 0:D], self.inp["abv"].ap(), {}),
                          (self.BROWB[0:1, D:D + 7], self.inp["frb"].ap(), {})],
                 w=(("browb", 0),))
        self.st("POOL", lambda e: e.memset(self.ONE1[:], 1.0), w=(("one1", 0),))
        self.st("POOL", lambda e: e.memset(self.ONE1F[:], 1.0), w=(("one1", 1),))
        self.dma("SP", [(self.IDF[:], self.inp["identf"].ap(), {})], w=(("idf", 0),))
        self.dma("POOL", [(self.FRW[:].rearrange("p (c k) -> p c k", k=7),
                           self.inp["frw"].ap().rearrange("(c p) k -> p c k", p=128), {})],
                 w=(("frw", 0),))
        self.st("DVE", lambda e: e.tensor_scalar(self.PAR[:, 1:2], self.PAR[:, 0:1],
                                                 -1.0, 1.0, ALU.mult, ALU.add),
                r=(("consts", 0),), w=(("consts", 1),))

    def load_gamma(self, na, nb=None):
        for i, name in enumerate([na] + ([nb] if nb else [])):
            self.dma("POOL", [(self.ROWB[:], self.inp[name].ap(), {})],
                     w=(("rowb", 0),))
            for hf in range(2):
                ps = self.PS[3]
                def emit(e, i=i, hf=hf, ps=ps):
                    last = None
                    for q in range(2):
                        last = e.matmul(ps[:, q * 512:(q + 1) * 512], self.ONE1[:],
                                        self.ROWB[0:1, hf * 1024 + q * 512:
                                                  hf * 1024 + (q + 1) * 512],
                                        start=True, stop=True)
                    return last
                self.st("PE", emit, r=(("rowb", 0), ("one1", 0)),
                        w=(("ps", 3, 0), ("ps", 3, 1)))
                self.st("DVE", lambda e, i=i, hf=hf, ps=ps: e.tensor_copy(
                    self.GAM[:, i * D + hf * 1024: i * D + (hf + 1) * 1024], ps[:]),
                    r=(("ps", 3, 0), ("ps", 3, 1)), w=(("gam", i),))

    def bn_mv(self, src_ap, rkeys, col, on_act=False, rms=False):
        MV = self.MV
        col = col % 4
        base = 24 * col
        mu = MV[:, 96 + 2 * col:97 + 2 * col]
        var = MV[:, 97 + 2 * col:98 + 2 * col]
        if on_act:
            dum = self.MISC[:, 4096 + (col % 2) * 2048: 4096 + (col % 2) * 2048 + 2048]
            dkey = ("dum", col % 2)
            if not rms:
                self.st("ACT", lambda e: e.activation(out=dum, in_=src_ap,
                                                      func=AF.Identity, accum_out=mu),
                        r=rkeys, w=(dkey, ("mv", col)))
            self.st("ACT", lambda e: e.activation(out=dum, in_=src_ap,
                                                  func=AF.Square, accum_out=var),
                    r=rkeys + (("mv", col),) if not rms else rkeys,
                    w=(dkey, ("mv", col)))
            if rms:
                return self.st("DVE", lambda e: e.tensor_scalar_mul(var, var, 1.0 / D),
                               r=(("mv", col),), w=(("mv", col),))
            def fin(e):
                e.tensor_scalar_mul(mu, mu, 1.0 / D)
                return e.tensor_scalar_mul(var, var, 1.0 / D)
            self.st("DVE", fin, r=(("mv", col),), w=(("mv", col),))
            ng = MV[:, 104 + col:105 + col]
            self.st("DVE", lambda e: e.tensor_scalar_mul(ng, mu, -1.0),
                    r=(("mv", col),), w=(("mv", col),))
            return self.st("DVE", lambda e: e.scalar_tensor_tensor(
                var, mu, ng, var, ALU.mult, ALU.add),
                r=(("mv", col),), w=(("mv", col),))
        def e1(e):
            for c in range(4):
                last = e.bn_stats(MV[:, base + 6 * c:base + 6 * (c + 1)],
                                  src_ap[:, 512 * c:512 * (c + 1)])
            return last
        self.st("DVE", e1, r=rkeys, w=(("mvscr", col),))
        return self.st("DVE", lambda e: e.bn_aggr(MV[:, 96 + 2 * col:98 + 2 * col],
                                                  MV[:, base:base + 24]),
                       r=(("mvscr", col),), w=(("mv", col),))

    def ln_scalars(self, col, rms=False):
        MV, SM1 = self.MV, self.SM1
        col = col % 4
        mu = MV[:, 96 + 2 * col:97 + 2 * col]
        var = MV[:, 97 + 2 * col:98 + 2 * col]
        nmu = SM1[:, 2 * col:2 * col + 1]
        rst = SM1[:, 2 * col + 1:2 * col + 2]
        if rms is True:
            def e1(e):
                e.memset(nmu, 0.0)
                return e.scalar_tensor_tensor(rst, mu, mu, var, ALU.mult, ALU.add)
        elif rms == "direct":
            def e1(e):
                e.memset(nmu, 0.0)
                return e.tensor_scalar(rst, var, EPS, None, ALU.add)
        else:
            def e1(e):
                e.tensor_scalar_mul(nmu, mu, -1.0)
                return e.tensor_scalar(rst, var, EPS, None, ALU.add)
        self.st("DVE", e1, r=(("mv", col),), w=(("sm1", col),))
        if rms is True:
            self.st("DVE", lambda e: e.tensor_scalar_add(rst, rst, EPS),
                    r=(("sm1", col),), w=(("sm1", col),))
        self.st("DVE", lambda e: e.reciprocal(rst, rst),
                r=(("sm1", col),), w=(("sm1", col),))
        return self.st("ACT", lambda e: e.activation(out=rst, in_=rst, func=AF.Sqrt),
                       r=(("sm1", col),), w=(("sm1", col),))

    def apply_ln(self, dst_ap, src_ap, col, rkeys, wkey):
        SM1 = self.SM1
        col = col % 4
        return self.st("DVE",
                       lambda e: e.tensor_scalar(dst_ap, src_ap,
                                                 SM1[:, 2 * col:2 * col + 1],
                                                 SM1[:, 2 * col + 1:2 * col + 2],
                                                 ALU.add, ALU.mult),
                       r=tuple(rkeys) + (("sm1", col),), w=(wkey,))

    def transpose_block(self, src_bf16, src_key, j, dst, dst_name, nch=16,
                        dst_colw=1024, dst_key=None):
        toks = []
        for half in range((nch + 7) // 8):
            ps = self.PS[3]
            pkey = [("ps", 3, 0), ("ps", 3, 1)]
            n = min(8, nch - half * 8)
            def emit_T(e, half=half, ps=ps, n=n):
                last = None
                for c in range(n):
                    cc = half * 8 + c
                    last = e.matmul(ps[:, c * 128:(c + 1) * 128],
                                    src_bf16[:, cc * 128:(cc + 1) * 128],
                                    self.IDENT[:], start=(c % 4 == 0),
                                    stop=(c % 4 == 3 or c == n - 1))
                return last
            self.st("PE", emit_T, r=(src_key, ("consts", 0)), w=pkey)
            def emit_ev(e, half=half, ps=ps, n=n, j=j):
                dv = dst[:, :].rearrange("p (c t) -> p c t", t=dst_colw)
                return e.tensor_copy(dv[:, half * 8:half * 8 + n, j * 128:(j + 1) * 128],
                                     ps[:, 0:n * 128].rearrange("p (c t) -> p c t", t=128))
            wk = dst_key if dst_key is not None else (dst_name, (half, j))
            wks = [wk] if not (isinstance(wk, tuple) and wk[0] == "stgp") else \
                  [("stg", 2 * wk[1]), ("stg", 2 * wk[1] + 1)]
            toks.append(self.st("DVE", emit_ev, r=pkey, w=wks))
        return toks

    def proj(self, wname, rows, cols, consume):
        W = self.inp[wname].ap()
        cw = min(cols, 512)
        nh = (cols + cw - 1) // cw
        for hidx in range(nh):
            slot = self.WA if hidx % 2 == 0 else self.WB
            wkey = ("Wslot", hidx % 2)
            wtok = self.load_whalf(W, rows, hidx * cw, min(cw, cols - hidx * cw),
                                   slot, wkey)
            consume(hidx, hidx * cw, min(cw, cols - hidx * cw), slot, wtok, wkey)

    def attn_a(self, feed_ffn=True):
        self.load_gamma("aln_g", "aln_b")
        for j in range(NB):
            self.norm_attn_a(j)
        self.kproj_fm("aWk", ("dram", self.agin_K, lambda o: [(ts(o), slice(0, 128))]),
                      bias_col0=16, half_keys=True)
        self.tK = self.s.cc(lambda e: e.collective_compute(
            "AllGather", ALU.bypass, replica_groups=RG,
            ins=[self.agin_K.ap().opt()], outs=[self.agout_K.ap().opt()]),
            reads=[("aginK", o) for o in range(16)], writes=(("agoutK", 0),))
        self.vproj_tm("aWv", self.agin_V, brow_bf16_off=0)
        self.tV = self.s.cc(lambda e: e.collective_compute(
            "AllGather", ALU.bypass, replica_groups=RG,
            ins=[self.agin_V.ap().opt()], outs=[self.agout_V.ap().opt()]),
            reads=[("aginV", j) for j in range(NB)], writes=(("agoutV", 0),))
        self.kproj_fm("aWq", ("dram", self.qtd, lambda o: [(ts(o), slice(0, 128))]),
                      bias_col0=0, dstkey="qtd", qscale=ISQ, half_keys=True)
        self.attn_core(scale=ISQ)
        if feed_ffn:
            self.s.alias([("Xn", 0), ("Xn", 1)],
                         self.keys_pref("prb", "QH", "kvdT", "qdT", "SG"))
            self.dma("POOL", [(self.BROWB[0:1, 0:D], self.inp["abo"].ap(), {})],
                     w=(("browb", 0),))
            self.load_gamma_pre_ffn()
            self.oproj("aWo", brow_bf16_off=0,
                       on_block_done=lambda j: self.normA_ln_rms(j, 0, spill=True))
        else:
            self.dma("POOL", [(self.BROWB[0:1, 0:D], self.inp["abo"].ap(), {})],
                     w=(("browb", 0),))
            self.oproj("aWo", brow_bf16_off=0)

    def norm_attn_a(self, j):
        R = self.STREAM[:, j * D:(j + 1) * D]
        p = j % 2
        N1 = self.MISC[:, p * 4096:(p + 1) * 4096].bitcast(F32)
        Xn = self.XNS[:, p * 2048:(p + 1) * 2048]
        rkey = ("R", j)
        self.bn_mv(R, (rkey,), j)
        self.heat()
        self.ln_scalars(j)
        self.apply_ln(N1[:], R, j, (rkey,), ("N1", p))
        self.bn_mv(N1[:], (("N1", p),), j)
        self.heat()
        self.ln_scalars(j)
        self.apply_ln(N1[:], N1[:], j, (("N1", p),), ("N1", p))
        self.st("DVE", lambda e: e.tensor_mul(N1[:], N1[:], self.GAM[:, 0:D]),
                r=(("N1", p), ("gam", 0)), w=(("N1", p),))
        self.st("DVE", lambda e: e.tensor_add(Xn[:], N1[:], self.GAM[:, D:2 * D]),
                r=(("N1", p), ("gam", 1)), w=(("Xn", p),))
        self.transpose_block(Xn[:], ("Xn", p), j, self.XT, "XT")

    def xt_keys(self):
        return [("XT", (h, j)) for h in range(2) for j in range(NB)]

    def kproj_fm(self, wname, dst, bias_col0=None, dstkey="aginK",
                 src_keys=None, src_arena=None, src_colw=1024, nd=16,
                 rows_in=None, cols=None, qscale=None, half_keys=False):
        src_arena = src_arena if src_arena is not None else self.XT
        src_keys = src_keys if src_keys is not None else self.xt_keys()
        rows_in = rows_in if rows_in is not None else D
        cols = cols if cols is not None else D
        def consume(hidx, c0, cw, slot, wtok, wkey):
            for ol in range(cw // 128):
                o = (c0 + ol * 128) // 128
                for h2 in range(2):
                    it = o * 2 + h2
                    ps = self.PS[it % 2]
                    pkey = ("ps", it % 2, 0)
                    def emit(e, ol=ol, h2=h2, ps=ps, slot=slot, cw=cw):
                        last = None
                        for d in range(nd):
                            last = e.matmul(
                                ps[:, 0:512],
                                slot[:, d * cw + ol * 128: d * cw + (ol + 1) * 128],
                                src_arena[:, d * src_colw + h2 * 512: d * src_colw + (h2 + 1) * 512],
                                start=(d == 0), stop=(d == nd - 1))
                        return last
                    if half_keys:
                        skeys = [("XT", (hf, j2)) for hf in range(2)
                                 for j2 in range(h2 * 4, h2 * 4 + 4)]
                    else:
                        skeys = list(src_keys)
                    self.st("PE", emit, r=[wkey] + skeys, w=(pkey,), x=(wtok,))
                    def mkev(tgt):
                        def ev(e, ps=ps, tgt=tgt, o=o):
                            if bias_col0 is not None and qscale is not None:
                                return e.tensor_scalar(tgt, ps[:, 0:512],
                                                       self.BIASC[:, bias_col0 + o:bias_col0 + o + 1],
                                                       qscale, ALU.add, ALU.mult)
                            if bias_col0 is not None:
                                return e.tensor_scalar_add(tgt, ps[:, 0:512],
                                                           self.BIASC[:, bias_col0 + o:bias_col0 + o + 1])
                            if qscale is not None:
                                return e.tensor_scalar_mul(tgt, ps[:, 0:512], qscale)
                            return e.tensor_copy(tgt, ps[:, 0:512])
                        return ev
                    if dst[0] == "sbuf":
                        tgt = dst[1](o, h2)
                        self.st("DVE", mkev(tgt), r=(pkey, ("biasc", 0)), w=((dstkey, o),))
                        continue
                    stg = self.STG[:, (it % 4) * 512: (it % 4) * 512 + 512]
                    self.st("DVE", mkev(stg), r=(pkey, ("biasc", 0)), w=(("stg", it % 4),))
                    dramdst, row_fn = dst[1], dst[2]
                    pairs = []
                    for (rsl, psl) in row_fn(o):
                        pairs.append((dramdst.ap()[rsl, h2 * 512:(h2 + 1) * 512],
                                      stg[psl, :], {}))
                    self.dma("SP", pairs, r=(("stg", it % 4),), w=((dstkey, o),))
        self.proj(wname, rows_in, cols, consume)

    def vproj_tm(self, wname, dramdst, brow_bf16_off=None,
                 src_keys=None, dstkey="aginV", cols=D,
                 src_arena=None, src_colw=1024, nd=16, rows_in=None):
        src_arena = src_arena if src_arena is not None else self.XT
        rows_in = rows_in if rows_in is not None else D
        def consume(hidx, c0, cw, slot, wtok, wkey):
            for j in range(NB):
                for r2 in range(cw // 512):
                    it = j * (cw // 512) + r2
                    ps = self.PS[it % 2]
                    pkey = ("ps", it % 2, 0)
                    def emit(e, j=j, r2=r2, ps=ps, slot=slot, cw=cw, c0=c0):
                        last = None
                        for d in range(nd):
                            last = e.matmul(
                                ps[:, 0:512],
                                src_arena[:, d * src_colw + j * 128: d * src_colw + (j + 1) * 128],
                                slot[:, d * cw + r2 * 512: d * cw + (r2 + 1) * 512],
                                start=(d == 0), stop=(brow_bf16_off is None and d == nd - 1))
                        if brow_bf16_off is None:
                            return last
                        return e.matmul(ps[:, 0:512], self.ONE1[:],
                                        self.BROWB[0:1, brow_bf16_off + c0 + r2 * 512:
                                                   brow_bf16_off + c0 + (r2 + 1) * 512],
                                        start=False, stop=True)
                    if src_keys is None:
                        skeys = [("XT", (hf, j)) for hf in range(2)]
                    else:
                        skeys = list(src_keys)
                    self.st("PE", emit, r=[wkey, ("one1", 0), ("browb", 0)] + skeys,
                            w=(pkey,), x=(wtok,))
                    stg = self.STG[:, (it % 4) * 512: (it % 4) * 512 + 512]
                    self.st("DVE", lambda e, ps=ps, stg=stg: e.tensor_copy(stg, ps[:, 0:512]),
                            r=(pkey,), w=(("stg", it % 4),))
                    self.dma("SP", [(dramdst.ap()[ts(j), c0 + r2 * 512: c0 + (r2 + 1) * 512],
                                     stg, {})],
                             r=(("stg", it % 4),), w=((dstkey, j),))
        self.proj(wname, rows_in, cols, consume)

    def load_head(self, h):
        hb = h % 2
        kcols = self.KH[:, hb * 2048:(hb + 1) * 2048]
        vcols = self.VH[:, hb * 2048:(hb + 1) * 2048]
        self.dma("SP", [(kcols[:, r * 1024:(r + 1) * 1024],
                         self.agout_K.ap()[r * 2048 + h * 128: r * 2048 + (h + 1) * 128, :], {})
                        for r in range(2)],
                 r=(("agoutK", 0),), w=(("KH", hb),), x=(self.tK,))
        self.dma("SP", [(vcols[:, r * 1024:(r + 1) * 1024].rearrange("p (s d) -> p s d", d=128),
                         self.agout_V.ap()[r * 1024:(r + 1) * 1024, h * 128:(h + 1) * 128]
                         .rearrange("(s p) d -> p s d", p=128), {})
                        for r in range(2)],
                 r=(("agoutV", 0),), w=(("VH", hb),), x=(self.tV,))
        qh = self.QTA[:, hb * 1024:(hb + 1) * 1024]
        self.dma("SP", [(qh, self.qtd.ap()[ts(h), :], {})],
                 r=[("qtd", o) for o in range(16)], w=(("QH", hb),))
        return kcols, vcols, qh

    def attn_core(self, scale):
        n = 16 * NB
        LAG = globals().get("ATTN_LAG", 2)
        ctx = {}
        hd = {}
        for i in range(n + LAG):
            if i < n:
                h, j = divmod(i, NB)
                if j == 0:
                    hd[h] = self.load_head(h)
                kcols, vcols, qh = hd[h]
                ctx[i] = self.attn_scores(h, j, h % 2, kcols, qh, i)
                self.attn_soft(ctx[i])
            if i >= LAG:
                self.attn_tav(ctx[i - LAG])
                del ctx[i - LAG]
        hd.clear()

    def attn_scores(self, h, j, hb, kcols, qh, it):
        s = self
        ke = (j + 1) * 128
        tot = 2 * ke
        nbuf = 1 if tot <= 1024 else 2
        bufs = []
        bidx = []
        for n in range(nbuf):
            i = [0, 1][self.rot3 % 2]
            bidx.append(i)
            bufs.append(self.PS[i])
            self.rot3 += 1
        pkeys = [("ps", i, hh) for i in bidx for hh in range(2)]

        mspans = [(j * 128, (j + 1) * 128), (ke + j * 128, ke + (j + 1) * 128)]
        plan = []
        for side in range(2):
            base, koff, pos = side * ke, side * 1024, 0
            while pos < ke:
                n = min(512, ke - pos, 512 - (base + pos) % 512)
                plan.append((base + pos, n, ("k", koff + pos)))
                pos += n
        for mi, (m0, _) in enumerate(mspans):
            plan.append((m0, 128, ("m", mi)))
        firsts, lasts = {}, {}
        for idx, (a0, n, _) in enumerate(plan):
            bk = a0 // 512
            firsts.setdefault(bk, idx)
            lasts[bk] = idx
        def emit_scores(e):
            last = None
            for idx, (a0, n, rd) in enumerate(plan):
                bk = a0 // 512
                b = bufs[a0 // 1024]
                dst = b[:, a0 % 1024:a0 % 1024 + n]
                if rd[0] == "k":
                    last = e.matmul(dst, qh[:, j * 128:(j + 1) * 128],
                                    kcols[:, rd[1]:rd[1] + n],
                                    start=(firsts[bk] == idx), stop=(lasts[bk] == idx))
                else:
                    mk = self.MK1 if rd[1] == 0 else self.MK2
                    last = e.matmul(dst, self.IDENT[:], mk[:],
                                    start=(firsts[bk] == idx), stop=(lasts[bk] == idx))
            return last
        s.st("PE", emit_scores,
             r=[("KH", hb), ("QH", hb), ("consts", 0)], w=pkeys)
        return dict(h=h, j=j, hb=hb, it=it, ke=ke, tot=tot, nbuf=nbuf,
                    bufs=bufs, bidx=bidx, pkeys=pkeys)

    def attn_soft(self, c):
        s = self
        h, j, hb, it = c["h"], c["j"], c["hb"], c["it"]
        tot, nbuf, bufs, pkeys = c["tot"], c["nbuf"], c["bufs"], c["pkeys"]
        p2 = it % 3
        nm = self.NM[:, p2 * 2: p2 * 2 + 2]
        def emit_max(e):
            last = None
            for n in range(nbuf):
                w = min(1024, tot - n * 1024)
                last = e.reduce_max(nm[:, n:n + 1], bufs[n][:, 0:w], axis=AX.X, negate=True)
            return last
        s.st("DVE", emit_max, r=pkeys, w=(("nm", p2),))
        if nbuf == 2:
            s.st("DVE", lambda e: e.tensor_tensor(nm[:, 0:1], nm[:, 0:1], nm[:, 1:2], ALU.min),
                 r=(("nm", p2),), w=(("nm", p2),))
        prb = self.PRB[:, p2 * 2048:(p2 + 1) * 2048]
        sm = self.SUMS[:, p2 * 2: p2 * 2 + 2]
        def emit_exp(e):
            last = None
            for n in range(nbuf):
                w = min(1024, tot - n * 1024)
                last = e.activation(out=prb[:, n * 1024:n * 1024 + w],
                                    in_=bufs[n][:, 0:w], func=AF.Exp,
                                    bias=nm[:, 0:1], scale=1.0,
                                    accum_out=sm[:, n:n + 1])
            return last
        s.st("ACT", emit_exp, r=pkeys + [("nm", p2)], w=(("prb", p2),))
        rc = self.RCP[:, p2:p2 + 1]
        dg = self.DG[:, p2 * 128:(p2 + 1) * 128]
        if nbuf == 2:
            s.st("DVE", lambda e: (e.tensor_add(rc, sm[:, 0:1], sm[:, 1:2])),
                 r=(("prb", p2),), w=(("rc", p2),))
            s.st("DVE", lambda e: e.reciprocal(rc, rc),
                 r=(("rc", p2),), w=(("rc", p2),))
        else:
            s.st("DVE", lambda e: e.reciprocal(rc, sm[:, 0:1]),
                 r=(("prb", p2),), w=(("rc", p2),))
        s.st("DVE", lambda e: e.tensor_scalar_mul(dg, self.IDENT[:], rc),
             r=(("rc", p2), ("consts", 0)), w=(("dg", p2),))

    def attn_tav(self, c):
        s = self
        h, j, hb, it = c["h"], c["j"], c["hb"], c["it"]
        tot, nbuf = c["tot"], c["nbuf"]
        p2 = it % 3
        pq = it % 2
        prb = self.PRB[:, p2 * 2048:(p2 + 1) * 2048]
        dg = self.DG[:, p2 * 128:(p2 + 1) * 128]
        vcols = self.VH[:, hb * 2048:(hb + 1) * 2048]
        prt = self.PRT[:, pq * 2048:(pq + 1) * 2048]
        nch = tot // 128
        nrnd = (nch + 3) // 4
        avi = 2 + it % 2
        av = self.PS[avi][:, 512:640]
        avkey = ("ps", avi, 1)
        def emit_avpart(rnd):
            n = min(4, nch - rnd * 4)
            def emit_av(e, rnd=rnd, n=n):
                last = None
                for c in range(rnd * 4, rnd * 4 + n):
                    if c <= j:
                        vt = vcols[:, c * 128:(c + 1) * 128]
                    else:
                        vt = vcols[:, 1024 + (c - (j + 1)) * 128: 1024 + (c - j) * 128]
                    last = e.matmul(av, vt, prt[:, c * 128:(c + 1) * 128],
                                    start=(c == 0), stop=(c == nch - 1))
                return last
            s.st("PE", emit_av, r=[("VH", hb), ("prt", (pq, rnd))], w=(avkey,))
        for rnd in range(nrnd):
            n = min(4, nch - rnd * 4)
            pti = 2 + rnd % 2
            pt = self.PS[pti]
            ptk = ("ps", pti, 0)
            def emit_T(e, rnd=rnd, n=n, pt=pt):
                last = None
                for c in range(n):
                    cc = rnd * 4 + c
                    last = e.matmul(pt[:, c * 128:(c + 1) * 128],
                                    prb[:, cc * 128:(cc + 1) * 128],
                                    dg, start=(c == 0), stop=(c == n - 1))
                return last
            s.st("PE", emit_T, r=(("prb", p2), ("dg", p2)), w=(ptk,))
            def emit_ev(e, rnd=rnd, n=n, pt=pt):
                dst = prt[:, rnd * 512:rnd * 512 + n * 128]
                if hasattr(e, "tensor_copy"):
                    return e.tensor_copy(dst, pt[:, 0:n * 128])
                return e.copy(dst, pt[:, 0:n * 128])
            s.st("DVE" if rnd % 2 == 0 else "ACT", emit_ev, r=(ptk,),
                 w=(("prt", (pq, rnd)),))
            if rnd >= 1:
                emit_avpart(rnd - 1)
        emit_avpart(nrnd - 1)
        ob = self.OSTG[:, hb * 1024: hb * 1024 + 1024]
        def emit_oev(e):
            return e.tensor_copy(ob[:, j * 128:(j + 1) * 128], av)
        s.st("DVE", emit_oev, r=(avkey,), w=(("ostg", (hb, j)),))
        if j == NB - 1:
            s.dma("SP", [(self.otd.ap()[ts(h), :], ob, {})],
                  r=[("ostg", (hb, jj)) for jj in range(NB)], w=(("otd", h),))

    def oproj(self, wname, brow_bf16_off, resid_arena=None, rname="R",
              on_block_done=None):
        resid_arena = resid_arena if resid_arena is not None else self.STREAM
        OJ = self.MISC
        def consume(hidx, c0, cw, slot, wtok, wkey):
            for j in range(NB):
                js = j % 2
                oj = OJ[:, js * 2048: js * 2048 + 2048]
                self.dma("SP", [(oj.rearrange("p (c k) -> p c k", k=128),
                                 self.otd.ap()[:, j * 128:(j + 1) * 128]
                                 .rearrange("(c p) k -> p c k", p=128), {})],
                         r=[("otd", hh) for hh in range(16)], w=(("oj", js),))
                for r2 in range(cw // 512):
                    it = j * (cw // 512) + r2
                    ps = self.PS[it % 2]
                    pkey = ("ps", it % 2, 0)
                    def emit(e, j=j, r2=r2, ps=ps, slot=slot, cw=cw, c0=c0, oj=oj):
                        last = None
                        for d in range(16):
                            last = e.matmul(
                                ps[:, 0:512],
                                oj[:, d * 128:(d + 1) * 128],
                                slot[:, d * cw + r2 * 512:d * cw + (r2 + 1) * 512],
                                start=(d == 0),
                                stop=(brow_bf16_off is None and d == 15))
                        if brow_bf16_off is None:
                            return last
                        return e.matmul(ps[:, 0:512], self.ONE1[:],
                                        self.BROWB[0:1, brow_bf16_off + c0 + r2 * 512:
                                                   brow_bf16_off + c0 + (r2 + 1) * 512],
                                        start=False, stop=True)
                    self.st("PE", emit, r=[wkey, ("one1", 0), ("browb", 0), ("oj", js)],
                            w=(pkey,), x=(wtok,))
                    dst = resid_arena[:, j * D + c0 + r2 * 512: j * D + c0 + (r2 + 1) * 512]
                    def ev(e, ps=ps, dst=dst):
                        return e.tensor_add(dst, dst, ps[:, 0:512])
                    self.st("DVE", ev, r=(pkey, (rname, j)), w=((rname, j),))
                    if on_block_done is not None and hidx == 3 and r2 == cw // 512 - 1:
                        on_block_done(j)
        self.proj(wname, D, D, consume)


    def keys_pref(self, *prefixes):
        return [k for k in list(self.s.state) if isinstance(k, tuple) and k[0] in prefixes]

    def normA_ln_rms(self, j, gam_idx, spill=False, ln=True, rms=True):
        R = self.STREAM[:, j * D:(j + 1) * D]
        rkey = ("R", j)
        xn = self.XNS[:, (j % 2) * 2048:(j % 2) * 2048 + 2048]
        if spill:
            self.dma("SP", [(self.rsp.ap()[ts(j), :], R, {})], r=(rkey,),
                     w=(("rsp", j),))
        if ln:
            self.bn_mv(R, (rkey,), j, on_act=True)
            self.heat()
            self.ln_scalars(j)
            if not rms:
                self.apply_ln(xn, R, j, (rkey,), ("Xn", j % 2))
                self.normB_to_xt(j)
                return
            self.apply_ln(R, R, j, (rkey,), rkey)
        self.bn_mv(R, (rkey,), j, on_act=True, rms=True)
        self.heat()
        self.ln_scalars(j, rms="direct")
        SM1 = self.SM1
        jm = j % 4
        self.st("DVE", lambda e: e.scalar_tensor_tensor(
            xn, R, SM1[:, 2 * jm + 1:2 * jm + 2],
            self.GAM[:, gam_idx * D:(gam_idx + 1) * D], ALU.mult, ALU.mult),
            r=(rkey, ("sm1", j % 4), ("gam", gam_idx)), w=(("Xn", j % 2),))
        self.normB_to_xt(j)

    def normB_to_xt(self, j):
        xn = self.XNS[:, (j % 2) * 2048:(j % 2) * 2048 + 2048]
        self.transpose_block(xn, ("Xn", j % 2), j, self.XT, "XT")

    def rope_proj(self, wname, dst_dram, dstkey_base, src_arena, src_colw, nd,
                  src_keys, tag):
        s = self
        kr32 = self.SCRa.bitcast(F32) if False else self.SCRa
        W = self.inp[wname].ap()
        wtoks, slots = [], []
        for hw in range(2):
            slot = self.WA if hw == 0 else self.WB
            wkey = ("Wslot", hw)
            wtoks.append(self.load_whalf(W, nd * 128, hw * 512, 512, slot, wkey))
            slots.append((slot, wkey))
        for j in range(NB):
            cskey = ("cs", tag)
            self.dma("SP", [(self.CS[:, 0:1024], self.inp["ropeC"].ap()[ts(j), :], {}),
                            (self.CS[:, 1024:2048], self.inp["ropeS"].ap()[ts(j), :], {})],
                     w=(cskey,))
            ps = self.PS[0]
            pk = [("ps", 0, 0), ("ps", 0, 1)]
            def emit_mm(e, j=j, ps=ps):
                last = None
                for hw in range(2):
                    slot = slots[hw][0]
                    for d in range(nd):
                        last = e.matmul(ps[:, hw * 512:(hw + 1) * 512],
                                        src_arena[:, d * src_colw + j * 128:
                                                  d * src_colw + (j + 1) * 128],
                                        slot[:, d * 512:(d + 1) * 512],
                                        start=(d == 0), stop=(d == nd - 1))
                return last
            s.st("PE", emit_mm, r=[slots[0][1], slots[1][1]] + list(src_keys),
                 w=pk, x=wtoks)
            s.st("DVE", lambda e, ps=ps: e.tensor_copy(self.SCRa[:, 0:1024], ps[:]),
                 r=pk, w=(("kr32", 0),))
            x3 = self.SCRa[:, 0:1024].rearrange("p (h i) -> p h i", i=64)
            t3 = self.SCRa[:, 1024:2048].rearrange("p (h i) -> p h i", i=64)
            c3 = self.CS[:, 0:1024].rearrange("p (h i) -> p h i", i=64)
            s3 = self.CS[:, 1024:2048].rearrange("p (h i) -> p h i", i=64)
            s.st("POOL", lambda e: e.tensor_mul(t3[:, :, 0:32], x3[:, :, 32:64], s3[:, :, 0:32]),
                 r=(("kr32", 0), cskey), w=(("krT", 0),))
            s.st("POOL", lambda e: e.tensor_mul(t3[:, :, 32:64], x3[:, :, 0:32], s3[:, :, 32:64]),
                 r=(("kr32", 0), cskey), w=(("krT", 1),))
            s.st("POOL", lambda e: e.tensor_mul(self.SCRa[:, 0:1024], self.SCRa[:, 0:1024],
                                                self.CS[:, 0:1024]),
                 r=(("kr32", 0), cskey, ("krT", 0), ("krT", 1)), w=(("kr32", 0),))
            s.st("POOL", lambda e: e.tensor_add(self.SCRb[:, 0:1024], self.SCRa[:, 0:1024],
                                                self.SCRa[:, 1024:2048]),
                 r=(("kr32", 0), ("krT", 0), ("krT", 1)), w=(("Xn", 0),))
            stg = self.STG[:, (j % 2) * 1024:(j % 2) * 1024 + 1024]
            self.transpose_block(self.SCRb[:, 0:1024], ("Xn", 0), 0, stg,
                                 "stg", nch=8, dst_colw=128,
                                 dst_key=("stgp", j % 2))
            pairs = []
            for c in range(8):
                pairs.append((dst_dram.ap()[(2 * c) * 128 + 64:(2 * c) * 128 + 128, ts(j)],
                              stg[0:64, c * 128:(c + 1) * 128], {}))
                pairs.append((dst_dram.ap()[(2 * c + 1) * 128 + 64:(2 * c + 1) * 128 + 128, ts(j)],
                              stg[64:128, c * 128:(c + 1) * 128], {}))
            self.dma("SP", pairs, r=[("stg", 2 * (j % 2)), ("stg", 2 * (j % 2) + 1)],
                     w=((dstkey_base, 8 + j),))

    def load_gamma_pre_ffn(self):
        s = self.s
        s.alias([("gam", 0), ("gam", 1), ("cs", 0), ("kr32", 0)],
                self.keys_pref("KH", "VH", "HE", "HT", "oj", "rold", "mk", "mu",
                               "mc", "mv", "bzt", "zero", "hsc", "hsc2", "agc",
                               "ags", "chc", "chox", "hint", "hin", "rowf",
                               "xn2", "N1", "cs", "gam", "dum"))
        self.load_gamma("fln1", "fln2")

    def ffn(self, tag, pre_normed=False):
        s = self.s
        if not pre_normed:
            self.load_gamma_pre_ffn()
            s.alias([("Xn", 0), ("Xn", 1)],
                    self.keys_pref("prb", "QH", "kvdT", "qdT", "SG", "cT"))
            for j in range(NB):
                self.normA_ln_rms(j, 0, spill=True)
        s.alias([("kvdT", c) for c in range(4)] + [("qdT", c) for c in range(4)]
                + [("stg", 0), ("stg", 1), ("ostg", (0, 0))],
                self.keys_pref(*QTA_PREFS))
        kvcol = lambda o, h2: self.QTA[:, o * 1024 + h2 * 512: o * 1024 + (h2 + 1) * 512]
        qdcol = lambda o, h2: self.QTA[:, 4096 + o * 1024 + h2 * 512: 4096 + o * 1024 + (h2 + 1) * 512]
        self.kproj_fm("fkvd", ("sbuf", kvcol), dstkey="kvdT", cols=512,
                      half_keys=True)
        self.kproj_fm("fqd", ("sbuf", qdcol), dstkey="qdT", cols=512, qscale=ISQ,
                      half_keys=True)
        kvdT = self.QTA[:, 0:4096]
        qdT = self.QTA[:, 4096:8192]
        kvkeys = [("kvdT", c) for c in range(4)]
        qdkeys = [("qdT", c) for c in range(4)]
        self.rope_proj("frk", self.agin_K, "aginK", self.XT, 1024, 16,
                       self.xt_keys(), tag)
        def rows_kp(o):
            return [(slice((2 * o) * 128, (2 * o) * 128 + 64), slice(0, 64)),
                    (slice((2 * o + 1) * 128, (2 * o + 1) * 128 + 64), slice(64, 128))]
        self.kproj_fm("fku", ("dram", self.agin_K, rows_kp), dstkey="aginK",
                      src_keys=kvkeys, src_arena=kvdT, src_colw=1024, nd=4,
                      rows_in=512, cols=1024)
        self.tK = s.cc(lambda e: e.collective_compute(
            "AllGather", ALU.bypass, replica_groups=RG,
            ins=[self.agin_K.ap().opt()], outs=[self.agout_K.ap().opt()]),
            reads=[("aginK", o) for o in range(16)], writes=(("agoutK", 0),))
        self.vproj_tm("fvu", self.agin_V, src_keys=kvkeys, src_arena=kvdT,
                      src_colw=1024, nd=4, rows_in=512)
        self.tV = s.cc(lambda e: e.collective_compute(
            "AllGather", ALU.bypass, replica_groups=RG,
            ins=[self.agin_V.ap().opt()], outs=[self.agout_V.ap().opt()]),
            reads=[("aginV", j) for j in range(NB)], writes=(("agoutV", 0),))
        self.kproj_fm("fqu", ("dram", self.qtd, rows_kp), dstkey="qtd",
                      src_keys=qdkeys, src_arena=qdT, src_colw=1024, nd=4,
                      rows_in=512, cols=1024)
        self.rope_proj("frq", self.qtd, "qtd", qdT, 1024, 4, qdkeys, tag + 1)
        s.alias([("KH", 0), ("KH", 1), ("VH", 0), ("VH", 1)],
                self.keys_pref("N1", "kr32", "dum"))
        s.alias([("QH", 0), ("QH", 1), ("prb", 0), ("prb", 1)]
                + [("prt", (p, r)) for p in range(2) for r in range(4)]
                + [("ostg", (h, j)) for h in range(2) for j in range(NB)],
                self.keys_pref("kvdT", "qdT"))
        self.attn_core(scale=ISQ)
        s.alias([("oj", 0), ("oj", 1)], self.keys_pref("KH"))
        s.alias([("Xn", 0), ("Xn", 1)], self.keys_pref("prb", "QH"))
        self.oproj("fow", brow_bf16_off=None,
                   on_block_done=lambda j: self.normA_ln_rms(j, 1, ln=False))
        self.router(tag)
        self.moe(tag)
        s.alias([("rold", 0)], self.keys_pref("HE", "HT"))
        for j in range(NB):
            self.dma("SP", [(self.SCRa[:, 0:D], self.rsp.ap()[ts(j), :], {})],
                     r=(("rsp", j),), w=(("rold", 0),))
            self.st("DVE", lambda e, j=j: e.tensor_add(
                self.STREAM[:, j * D:(j + 1) * D],
                self.STREAM[:, j * D:(j + 1) * D], self.SCRa[:, 0:D]),
                r=(("rold", 0), ("R", j)), w=(("R", j),))

    def router(self, tag):
        s = self
        RT, WS, ONES7 = self.RT, self.WS, self.RT[:, 56:63]
        self.st("DVE", lambda e: e.memset(ONES7, 1.0), w=(("ones7", 0),))
        for j in range(NB):
            ps = self.PS[0]
            pk = ("ps", 0, 0)
            def emit(e, j=j, ps=ps):
                for d in range(16):
                    e.matmul(ps[:, 0:7],
                             self.XT[:, d * 1024 + j * 128: d * 1024 + (j + 1) * 128],
                             self.FRW[:, d * 7:(d + 1) * 7],
                             start=(d == 0), stop=False)
                return e.matmul(ps[:, 0:7], self.ONE1[:],
                                self.BROWB[0:1, D:D + 7],
                                start=False, stop=True)
            s.st("PE", emit, r=[("frw", 0), ("one1", 0), ("browb", 0)] + self.xt_keys(),
                 w=(pk,))
            p = RT[:, 0:7]
            s.st("ACT", lambda e, ps=ps, p=p: e.activation(out=p, in_=ps[:, 0:7],
                                                           func=AF.Sigmoid),
                 r=(pk,), w=(("rt", 0),))
            m1 = RT[:, 7:8]; eq = RT[:, 8:15]; cum = RT[:, 15:22]
            m1f = RT[:, 22:29]; pp2 = RT[:, 29:36]; m2 = RT[:, 36:37]
            eq2 = RT[:, 37:44]; cum2 = RT[:, 44:51]; f2 = RT[:, 51:52]
            t1 = RT[:, 52:53]; den = RT[:, 53:54]; rec = RT[:, 54:55]
            D_ = [("rt", 0)]
            def dve(fn, rk, wk):
                return s.st("DVE", fn, r=rk, w=wk)
            dve(lambda e: e.reduce_max(m1, p, axis=AX.X), D_, [("rt", 1)])
            dve(lambda e: e.tensor_single_scalar(eq, p, m1, ALU.is_equal),
                D_ + [("rt", 1)], [("rt", 2)])
            dve(lambda e: e.tensor_tensor_scan(cum, ONES7, eq, 0.0, ALU.mult, ALU.add),
                [("rt", 2), ("ones7", 0)], [("rt", 3)])
            dve(lambda e: e.tensor_single_scalar(cum, cum, 1.0, ALU.is_equal),
                [("rt", 3)], [("rt", 3)])
            dve(lambda e: e.tensor_mul(m1f, eq, cum), [("rt", 2), ("rt", 3)], [("rt", 4)])
            dve(lambda e: e.scalar_tensor_tensor(pp2, m1f, -1e9, p, ALU.mult, ALU.add),
                [("rt", 4)] + D_, [("rt", 5)])
            dve(lambda e: e.reduce_max(m2, pp2, axis=AX.X), [("rt", 5)], [("rt", 6)])
            dve(lambda e: e.tensor_single_scalar(eq2, pp2, m2, ALU.is_equal),
                [("rt", 5), ("rt", 6)], [("rt", 7)])
            dve(lambda e: e.tensor_tensor_scan(cum2, ONES7, eq2, 0.0, ALU.mult, ALU.add),
                [("rt", 7), ("ones7", 0)], [("rt", 8)])
            dve(lambda e: e.tensor_single_scalar(cum2, cum2, 1.0, ALU.is_equal),
                [("rt", 8)], [("rt", 8)])
            dve(lambda e: e.tensor_mul(eq2, eq2, cum2), [("rt", 7), ("rt", 8)], [("rt", 9)])
            dve(lambda e: e.tensor_add(m1f, m1f, eq2), [("rt", 4), ("rt", 9)], [("rt", 10)])
            dve(lambda e: e.tensor_add(den, m1, m2), [("rt", 1), ("rt", 6)], [("rt", 11)])
            dve(lambda e: e.reciprocal(rec, den), [("rt", 11)], [("rt", 12)])
            dve(lambda e: e.tensor_scalar_mul(rec, rec, 0.125), [("rt", 12)], [("rt", 12)])
            dve(lambda e: e.tensor_mul(eq, p, m1f), D_ + [("rt", 10)], [("rt", 13)])
            dve(lambda e, j=j: e.tensor_scalar_mul(self.WS[:, j * 8 + 1:j * 8 + 8],
                                                   eq, rec),
                [("rt", 13), ("rt", 12)], [("ws", j)])
            dve(lambda e, j=j: e.memset(self.WS[:, j * 8:j * 8 + 1], 0.125),
                [], [("ws", j)])

    def moe(self, tag, moe_cb=None):
        s = self.s
        HE = self.MISC[:, 0:8192]
        HT = self.MISC[:, 8192:16384]
        s.alias([("HE", j) for j in range(NB)] + [("HT", j) for j in range(NB)],
                self.keys_pref(*MISC_PREFS))
        s.alias([("SG", 0), ("SG", 1)], self.keys_pref(*QTA_PREFS))
        for e_ in range(8):
            wg = self.inp["fshg"].ap() if e_ == 0 else self.inp["frg"].ap()[e_ - 1]
            wu = self.inp["fshu"].ap() if e_ == 0 else self.inp["fru"].ap()[e_ - 1]
            wd = self.inp["fshd"].ap() if e_ == 0 else self.inp["frd"].ap()[e_ - 1]
            for hd in range(2):
                tg = self.load_whalf(wg, D, hd * 512, 512, self.WA, ("Wslot", 0))
                tu = self.load_whalf(wu, D, hd * 512, 512, self.WB, ("Wslot", 1))
                for j in range(NB):
                    ps = self.PS[j % 2]
                    pklo = ("ps", j % 2, 0)
                    pkhi = ("ps", j % 2, 1)
                    def emit_g(e, j=j, ps=ps):
                        last = None
                        for d in range(16):
                            last = e.matmul(ps[:, 0:512],
                                            self.XT[:, d * 1024 + j * 128: d * 1024 + (j + 1) * 128],
                                            self.WA[:, d * 512:(d + 1) * 512],
                                            start=(d == 0), stop=(d == 15))
                        return last
                    s.step("PE", emit_g, reads=[("Wslot", 0)] + self.xt_keys(),
                           writes=(pklo,), extra_deps=(tg,))
                    sg = self.QTA[:, (j % 2) * 512:(j % 2) * 512 + 512]
                    self.st("ACT", lambda e, ps=ps, sg=sg: e.activation(
                        out=sg, in_=ps[:, 0:512], func=AF.Silu),
                        r=(pklo,), w=(("SG", j % 2),))
                    def emit_u(e, j=j, ps=ps):
                        last = None
                        for d in range(16):
                            last = e.matmul(ps[:, 512:1024],
                                            self.XT[:, d * 1024 + j * 128: d * 1024 + (j + 1) * 128],
                                            self.WB[:, d * 512:(d + 1) * 512],
                                            start=(d == 0), stop=(d == 15))
                        return last
                    s.step("PE", emit_u, reads=[("Wslot", 1)] + self.xt_keys(),
                           writes=(pkhi,), extra_deps=(tu,))
                    he = HE[:, j * 1024 + hd * 512: j * 1024 + (hd + 1) * 512]
                    self.st("DVE", lambda e, j=j, ps=ps, sg=sg, he=he, e_=e_:
                            e.scalar_tensor_tensor(he, sg, self.WS[:, j * 8 + e_:j * 8 + e_ + 1],
                                                   ps[:, 512:1024], ALU.mult, ALU.mult),
                            r=(("SG", j % 2), pkhi, ("ws", j)), w=(("HE", j),))
            for ho in range(2):
                slot = self.WA if ho == 0 else self.WB
                td = self.load_whalf(wd, 1024, ho * 1024, 1024, slot, ("Wslot", ho))
                def emit_down(j, slot=slot, ho=ho, td=td, e_=e_):
                    def emit_d(e, j=j, slot=slot):
                        last = None
                        for r2 in range(2):
                            for c in range(8):
                                last = e.matmul(self.PS[3][:, r2 * 512:(r2 + 1) * 512],
                                                HT[:, j * 1024 + c * 128: j * 1024 + (c + 1) * 128],
                                                slot[:, c * 1024 + r2 * 512: c * 1024 + (r2 + 1) * 512],
                                                start=(c == 0), stop=(c == 7))
                        return last
                    self.st("PE", emit_d, r=(("HT", j), ("Wslot", ho)),
                            w=(("ps", 3, 0), ("ps", 3, 1)), x=(td,))
                    def emit_a(e, j=j, ho=ho):
                        return e.tensor_add(
                            self.STREAM[:, j * D + ho * 1024: j * D + (ho + 1) * 1024],
                            self.STREAM[:, j * D + ho * 1024: j * D + (ho + 1) * 1024],
                            self.PS[3][:])
                    self.st("DVE", emit_a,
                            r=(("ps", 3, 0), ("ps", 3, 1), ("R", j)), w=(("R", j),))
                    if moe_cb is not None and e_ == 7 and ho == 1:
                        moe_cb(j)
                for j in range(NB + (2 if ho == 0 else 0)):
                    if ho == 0 and j >= 2:
                        emit_down(j - 2)
                    if ho == 0 and j < NB:
                        def emit_T(e, j=j):
                            last = None
                            for c in range(8):
                                last = e.matmul(self.PS[2][:, c * 128:(c + 1) * 128],
                                                HE[:, j * 1024 + c * 128: j * 1024 + (c + 1) * 128],
                                                self.IDENT[:], start=(c % 4 == 0),
                                                stop=(c % 4 == 3))
                            return last
                        self.st("PE", emit_T, r=(("HE", j), ("consts", 0)),
                                w=(("ps", 2, 0), ("ps", 2, 1)))
                        self.st("DVE", lambda e, j=j: e.tensor_copy(
                            HT[:, j * 1024:(j + 1) * 1024], self.PS[2][:]),
                            r=(("ps", 2, 0), ("ps", 2, 1)), w=(("HT", j),))
                    elif ho == 1:
                        emit_down(j)


    def scan_block(self):
        s = self.s
        M = self.MISC.bitcast(F32)
        Mk = [M[:, 0:512], M[:, 4992:5504]]
        Mu = [M[:, 512:1024], M[:, 5504:6016]]
        Mc = [M[:, 1024:1536], M[:, 6016:6528]]
        Mv = [M[:, 1536:2048], M[:, 6528:7040]]
        BZT, BNT = M[:, 2048:2560], M[:, 2560:3072]
        HSC = M[:, 3072:3584]
        HSCB = M[:, 7168:7680]
        ZERO = M[:, 3584:3712]
        AGC, AGS = M[:, 3712:3776], M[:, 3776:3840]
        CHC, CHS = M[:, 3840:3856], M[:, 3856:3872]
        CHOX = M[:, 3872:3889]
        HINT = M[:, 3889:3897]
        HIN = M[:, 3897:3905]
        QF = self.QTA.bitcast(F32)
        mkeys = ([("mk", 0), ("mu", 0), ("mc", 0), ("mv", 0), ("bzt", 0),
                  ("zero", 0), ("agc", 0), ("ags", 0), ("chc", 0), ("chox", 0),
                  ("hint", 0), ("hin", 0), ("rowf", 0), ("xn2", 0)]
                 + [("hsc", c) for c in range(4)]
                 + [("hsc2", (c, j)) for c in range(4) for j in range(NB)])
        s.alias(mkeys, self.keys_pref(*MISC_PREFS))
        s.alias([("cT", (c, j)) for c in range(4) for j in range(NB)],
                self.keys_pref(*QTA_PREFS))
        self.st("POOL", lambda e: e.memset(ZERO, 0.0), w=(("zero", 0),))
        for j in range(NB):
            R = self.STREAM[:, j * D:(j + 1) * D]
            self.bn_mv(R, (("R", j),), 3)
            self.ln_scalars(3)
            self.apply_ln(self.SCRb2[:], R, 3, (("R", j),), ("xn2", 0))
            self.transpose_block(self.SCRb2[:], ("xn2", 0), j, self.XT, "XT")
        self.s.alias([("hscB", c) for c in range(4)], self.keys_pref("xn2", "Xn"))
        for q in range(4):
            self.scan_quarter(q, M, Mk, Mu, Mc, Mv, BZT, BNT, HSC, HSCB, ZERO,
                              AGC, AGS, CHC, CHS, CHOX, HINT, HIN, QF)

    def scan_quarter(self, q, M, Mk, Mu, Mc, Mv, BZT, BNT, HSC, HSCB, ZERO,
                     AGC, AGS, CHC, CHS, CHOX, HINT, HIN, QF):
        s = self.s
        qc = q * 512
        tz = self.load_whalf(self.inp["bWzT"].ap(), D, qc, 512, self.WA, ("Wslot", 0))
        tn = self.load_whalf(self.inp["bWnT"].ap(), D, qc, 512, self.WB, ("Wslot", 1))
        self.dma("SP", [(self.ROWF[0:1, 0:512], self.inp["bbz"].ap()[0:1, qc:qc + 512], {}),
                        (self.ROWF[0:1, 512:1024], self.inp["bbn"].ap()[0:1, qc:qc + 512], {})],
                 w=(("rowf", 0),))
        def emit_bz(e):
            e.matmul(self.PS[2][:, 0:512], self.ONE1F[:], self.ROWF[0:1, 0:512],
                     start=True, stop=True)
            return e.matmul(self.PS[2][:, 512:1024], self.ONE1F[:],
                            self.ROWF[0:1, 512:1024], start=True, stop=True)
        self.st("PE", emit_bz, r=(("rowf", 0), ("one1", 1)),
                w=(("ps", 2, 0), ("ps", 2, 1)))
        self.st("DVE", lambda e: (e.tensor_copy(BZT, self.PS[2][:, 0:512]),
                                  e.tensor_copy(BNT, self.PS[2][:, 512:1024]))[-1],
                r=(("ps", 2, 0), ("ps", 2, 1), ("mz", 0)), w=(("bzt", 0),))
        def p1_mm_el(jj):
            p = jj % 2
            mk_, mu_, mc_, mv_ = Mk[p], Mu[p], Mc[p], Mv[p]
            ps = self.PS[p]
            pklo, pkhi = ("ps", p, 0), ("ps", p, 1)
            def emit_k(e, ps=ps, jj=jj):
                last = None
                for d in range(16):
                    last = e.matmul(ps[:, 0:512],
                                    self.XT[:, d * 1024 + jj * 128: d * 1024 + (jj + 1) * 128],
                                    self.WA[:, d * 512:(d + 1) * 512],
                                    start=(d == 0), stop=(d == 15))
                return last
            def emit_u(e, ps=ps, jj=jj):
                last = None
                for d in range(16):
                    last = e.matmul(ps[:, 512:1024],
                                    self.XT[:, d * 1024 + jj * 128: d * 1024 + (jj + 1) * 128],
                                    self.WB[:, d * 512:(d + 1) * 512],
                                    start=(d == 0), stop=(d == 15))
                return last
            s.step("PE", emit_k, reads=[("Wslot", 0)] + self.xt_keys(), writes=(pklo,),
                   extra_deps=(tz,))
            s.step("PE", emit_u, reads=[("Wslot", 1)] + self.xt_keys(), writes=(pkhi,),
                   extra_deps=(tn,))
            self.st("DVE", lambda e: e.tensor_add(mk_, ps[:, 0:512], BZT),
                    r=(pklo, ("bzt", 0)), w=(("mk", p),))
            self.st("DVE", lambda e: e.tensor_add(mu_, ps[:, 512:1024], BNT),
                    r=(pkhi, ("bzt", 0)), w=(("mu", p),))
            self.st("ACT", lambda e: e.activation(out=mc_, in_=mk_, func=AF.Sigmoid,
                                                  scale=-1.0),
                    r=(("mk", p),), w=(("mc", p),))
            self.st("ACT", lambda e: e.activation(out=mv_, in_=mu_, func=AF.Sigmoid),
                    r=(("mu", p),), w=(("mv", p),))
            self.st("DVE", lambda e: e.tensor_scalar_add(mk_, mu_, 0.5),
                    r=(("mu", p), ("mk", p)), w=(("mk", p),))
            self.st("DVE", lambda e: e.tensor_max(mv_, mv_, mk_),
                    r=(("mv", p), ("mk", p)), w=(("mv", p),))
            self.st("DVE", lambda e: e.tensor_scalar(mk_, mc_, -1.0, 1.0, ALU.mult, ALU.add),
                    r=(("mc", p), ("mk", p)), w=(("mk", p),))
            self.st("DVE", lambda e: e.tensor_mul(mv_, mv_, mk_),
                    r=(("mv", p), ("mk", p)), w=(("mv", p),))

        def p1_T(jj):
            p = jj % 2
            mc_, mv_ = Mc[p], Mv[p]
            def emit_Tc(e, jj=jj):
                last = None
                for c in range(4):
                    last = e.matmul(self.PS[2][:, c * 128:(c + 1) * 128],
                                    mc_[:, c * 128:(c + 1) * 128], self.IDF[:],
                                    start=(c == 0), stop=(c == 3))
                for c in range(4):
                    last = e.matmul(self.PS[2][:, 512 + c * 128:512 + (c + 1) * 128],
                                    mv_[:, c * 128:(c + 1) * 128], self.IDF[:],
                                    start=(c == 0), stop=(c == 3))
                return last
            self.st("PE", emit_Tc, r=(("mc", p), ("mv", p), ("idf", 0)),
                    w=(("ps", 2, 0), ("ps", 2, 1)))
            def emit_ev(e, jj=jj):
                e.tensor_copy(QF[:, :4096].rearrange("p (cc j k) -> p cc j k", j=NB, k=128)[:, :, jj, :],
                              self.PS[2][:, 0:512].rearrange("p (cc k) -> p cc k", k=128))
                return e.tensor_copy(QF[:, 4096:].rearrange("p (cc j k) -> p cc j k", j=NB, k=128)[:, :, jj, :],
                                     self.PS[2][:, 512:1024].rearrange("p (cc k) -> p cc k", k=128))
            self.st("DVE", emit_ev, r=(("ps", 2, 0), ("ps", 2, 1)),
                    w=[("cT", (cc2, jj)) for cc2 in range(4)])
            for cc in range(4):
                ct = QF[:, (cc * NB + jj) * 128:(cc * NB + jj + 1) * 128]
                vt = QF[:, 4096 + (cc * NB + jj) * 128:4096 + (cc * NB + jj + 1) * 128]
                self.st("DVE", lambda e, ct=ct, vt=vt, cc=cc:
                        e.tensor_tensor_scan(HSC[:, cc * 128:(cc + 1) * 128], ct, vt,
                                             0.0, ALU.mult, ALU.add),
                        r=(("cT", (cc, jj)), ("zero", 0)), w=(("hsc", cc),))
                self.st("DVE", lambda e, cc=cc, jj=jj:
                        e.tensor_copy(AGS[:, cc * NB + jj:cc * NB + jj + 1],
                                      HSC[:, (cc + 1) * 128 - 1:(cc + 1) * 128]),
                        r=(("hsc", cc),), w=(("ags", 0),))
                self.st("DVE", lambda e, ct=ct, cc=cc:
                        e.tensor_tensor_scan(HSC[:, cc * 128:(cc + 1) * 128], ct,
                                             ZERO, 1.0, ALU.mult, ALU.add),
                        r=(("cT", (cc, jj)), ("zero", 0), ("hsc", cc)), w=(("hsc", cc),))
                self.st("DVE", lambda e, cc=cc, jj=jj:
                        e.tensor_copy(AGC[:, cc * NB + jj:cc * NB + jj + 1],
                                      HSC[:, (cc + 1) * 128 - 1:(cc + 1) * 128]),
                        r=(("hsc", cc),), w=(("agc", 0),))

        for jj in range(NB + 1):
            if jj < NB:
                p1_mm_el(jj)
            if jj >= 1:
                p1_T(jj - 1)
        self.dma("SP", [(self.agin_S.ap()[:, 0:64], AGC, {}),
                        (self.agin_S.ap()[:, 64:128], AGS, {})],
                 r=(("agc", 0), ("ags", 0)), w=(("aginS", 0),))
        tS = s.cc(lambda e: e.collective_compute(
            "AllGather", ALU.bypass, replica_groups=RG,
            ins=[self.agin_S.ap().opt()], outs=[self.agout_S.ap().opt()]),
            reads=(("aginS", 0),), writes=(("agoutS", 0),))
        for cc in range(4):
            nck = {"allow_slow_non_contiguous": True}
            self.dma("SP", [(CHC[:, 0:16:2], self.agout_S.ap()[0:128, cc * 8:(cc + 1) * 8], nck),
                            (CHC[:, 1:16:2], self.agout_S.ap()[128:256, cc * 8:(cc + 1) * 8], nck),
                            (CHS[:, 0:16:2], self.agout_S.ap()[0:128, 64 + cc * 8:64 + (cc + 1) * 8], nck),
                            (CHS[:, 1:16:2], self.agout_S.ap()[128:256, 64 + cc * 8:64 + (cc + 1) * 8], nck)],
                 r=(("agoutS", 0),), w=(("chc", 0),), x=(tS,))
            self.st("DVE", lambda e: e.memset(CHOX[:, 0:1], 0.5),
                    r=(("chc", 0),), w=(("chox", 0),))
            self.st("DVE", lambda e: e.tensor_tensor_scan(CHOX[:, 1:17], CHC, CHS,
                                                          0.5, ALU.mult, ALU.add),
                    r=(("chc", 0), ("chox", 0)), w=(("chox", 0),))
            self.st("DVE", lambda e: e.tensor_single_scalar(HINT, CHOX[:, 0:16:2],
                                                            self.PAR[:, 1:2], ALU.mult),
                    r=(("chox", 0), ("consts", 1)), w=(("hint", 0),))
            self.st("DVE", lambda e: e.scalar_tensor_tensor(HIN, CHOX[:, 1:17:2],
                                                            self.PAR[:, 0:1], HINT,
                                                            ALU.mult, ALU.add),
                    r=(("chox", 0), ("hint", 0), ("consts", 0)), w=(("hin", 0),))
            def p2_scan(jj, cc=cc):
                p = jj % 2
                hb = HSC if p == 0 else HSCB
                hk = ("hsc", cc) if p == 0 else ("hscB", cc)
                ct = QF[:, (cc * NB + jj) * 128:(cc * NB + jj + 1) * 128]
                vt = QF[:, 4096 + (cc * NB + jj) * 128:4096 + (cc * NB + jj + 1) * 128]
                self.st("DVE", lambda e, ct=ct, vt=vt:
                        e.tensor_tensor_scan(hb[:, cc * 128:(cc + 1) * 128], ct, vt,
                                             HIN[:, jj:jj + 1], ALU.mult, ALU.add),
                        r=(("cT", (cc, jj)), ("hin", 0), hk), w=(("hsc2", (cc, jj)), hk))
            def p2_tadd(jj, cc=cc):
                p = jj % 2
                hb = HSC if p == 0 else HSCB
                def emit_Th(e, cc=cc, jj=jj):
                    return e.matmul(self.PS[3][:, cc * 128:(cc + 1) * 128],
                                    hb[:, cc * 128:(cc + 1) * 128], self.IDF[:],
                                    start=True, stop=True)
                self.st("PE", emit_Th, r=(("hsc2", (cc, jj)), ("idf", 0)),
                        w=(("ps", 3, 0),))
                self.st("DVE", lambda e, cc=cc, jj=jj: e.tensor_add(
                    self.STREAM[:, jj * D + q * 512 + cc * 128: jj * D + q * 512 + (cc + 1) * 128],
                    self.STREAM[:, jj * D + q * 512 + cc * 128: jj * D + q * 512 + (cc + 1) * 128],
                    self.PS[3][:, cc * 128:(cc + 1) * 128]),
                    r=(("ps", 3, 0), ("R", jj)), w=(("R", jj),))
            for jj in range(NB + 1):
                if jj < NB:
                    p2_scan(jj)
                if jj >= 1:
                    p2_tadd(jj - 1)

    def phase_out(self):
        toks = []
        for j in range(NB):
            toks.append(self.dma("SP", [(self.out.ap()[ts(j), :],
                                         self.STREAM[:, j * D:(j + 1) * D], {})],
                                 r=(("R", j),), w=(("out", j),)))
        self.s.barrier_wait("SP", toks)

    def build(self):
        self.phase_init()
        import os
        fh = os.environ.get("MK_FEED", "1") == "1"
        self.attn_a(feed_ffn=fh and self.stop in ("x2", "x3", "x4"))
        if self.stop in ("x2", "x3", "x4"):
            self.ffn(tag=10, pre_normed=fh)
        if self.stop in ("x3", "x4"):
            self.scan_block()
        if self.stop == "x4":
            self.ffn(tag=20)
        self.phase_out()
        self.s.build()
        return self.nc


NAMEMAP = {
    "aWq": "L0_attn_a_Wq", "aWk": "L0_attn_a_Wk", "aWv": "L0_attn_a_Wv",
    "aWo": "L0_attn_a_Wo",
    "aln_g": "L0_attn_a_ln_g", "aln_b": "L0_attn_a_ln_b",
    "fln1": "L0_ffn_ln1_w", "fln2": "L0_ffn_ln2_w",
    "fkvd": "L0_ffn_kv_d_w", "fqd": "L0_ffn_q_d_w",
    "fku": "L0_ffn_k_u_w", "fvu": "L0_ffn_v_u_w", "fqu": "L0_ffn_q_u_w",
    "frk": "L0_ffn_rope_k_w", "frq": "L0_ffn_rope_q_w", "fow": "L0_ffn_o_w",
    "fshg": "L0_ffn_sh_gate", "fshu": "L0_ffn_sh_up", "fshd": "L0_ffn_sh_down",
    "frg": "L0_ffn_r_gate", "fru": "L0_ffn_r_up", "frd": "L0_ffn_r_down",
    "frw": "L0_ffn_router_w",
}


def make_inmaps(inputs, n_cores=8):
    X = np.asarray(inputs["L0_attn_a_X"], np.float32)
    shared = {}
    for k, ref in NAMEMAP.items():
        shared[k] = np.ascontiguousarray(np.asarray(inputs[ref], np.float32))
    shared["abq"] = np.ascontiguousarray(np.asarray(inputs["L0_attn_a_bq"], np.float32).reshape(16, 128).T)
    shared["abk"] = np.ascontiguousarray(np.asarray(inputs["L0_attn_a_bk"], np.float32).reshape(16, 128).T)
    shared["abv"] = np.asarray(inputs["L0_attn_a_bv"], np.float32).reshape(1, D)
    shared["abo"] = np.asarray(inputs["L0_attn_a_bo"], np.float32).reshape(1, D)
    shared["aln_g"] = shared["aln_g"].reshape(1, D)
    shared["aln_b"] = shared["aln_b"].reshape(1, D)
    shared["fln1"] = shared["fln1"].reshape(1, D)
    shared["fln2"] = shared["fln2"].reshape(1, D)
    shared["frb"] = np.asarray(inputs["L0_ffn_router_b"], np.float32).reshape(1, 7)
    shared["bWzT"] = np.ascontiguousarray(
        np.asarray(inputs["L0_attn_b_Wz"], np.float32).T)
    shared["bWnT"] = np.ascontiguousarray(
        np.asarray(inputs["L0_attn_b_Wn"], np.float32).T)
    shared["bbz"] = np.asarray(inputs["L0_attn_b_bz"], np.float32).reshape(1, D)
    shared["bbn"] = np.asarray(inputs["L0_attn_b_bn"], np.float32).reshape(1, D)
    shared["ident"] = np.eye(128, dtype=BF)
    shared["identf"] = np.eye(128, dtype=np.float32)
    ar = np.arange(128)
    diag = np.where(ar[None, :] > ar[:, None], np.float32(-1e30), np.float32(0)).astype(BF)
    allneg = np.full((128, 128), -1e30, np.float32).astype(BF)
    zeros = np.zeros((128, 128), BF)

    in_maps = []
    for c in range(n_cores):
        b, r = c // 2, c % 2
        m = dict(shared)
        blocks = [2 * j + r for j in range(NB)]
        m["x0"] = np.ascontiguousarray(
            X[b].reshape(16, 128, D)[blocks].reshape(TOK, D))
        m["mask1"] = diag if r == 0 else zeros
        m["mask2"] = allneg if r == 0 else diag
        pos = (np.asarray(blocks)[:, None] * 128 + ar[None, :]).reshape(TOK)
        inv_freq = 1.0 / 10000.0 ** (np.arange(0, 64, 2, np.float32) / 64)
        ang = np.outer(pos.astype(np.float32), inv_freq)
        cos = np.cos(np.concatenate([ang, ang], 1))
        sin = np.sin(ang)
        sa = np.concatenate([-sin, sin], 1)
        m["ropeC"] = np.ascontiguousarray(np.tile(cos, (1, 16)).astype(np.float32))
        m["ropeS"] = np.ascontiguousarray(np.tile(sa, (1, 16)).astype(np.float32))
        m["par"] = np.full((128, 1), float(r), np.float32)
        in_maps.append(m)
    return in_maps


def reassemble(res, n_cores=8):
    Y = np.zeros((n_cores // 2, 2048, D), np.float32)
    for c in range(n_cores):
        b, r = c // 2, c % 2
        y = np.asarray(res[c]["y"]).reshape(NB, 128, D)
        for j in range(NB):
            Y[b, (2 * j + r) * 128:(2 * j + r + 1) * 128] = y[j]
    return Y




_CACHE = {}


def kernel(**inputs):
    n_cores = 8
    if "nc" not in _CACHE:
        nc = bass.Bass("TRN2", target_bir_lowering=False)
        m = MK(nc, stop_after="x4")
        m.build()
        _CACHE["nc"] = nc
    in_maps = make_inmaps(inputs, n_cores)
    res = run_bass_kernel_spmd(_CACHE["nc"], in_maps, core_ids=list(range(n_cores)))
    return reassemble(res.results, n_cores)


# version 10
# speedup vs baseline: 1.0031x; 1.0031x over previous
import numpy as np
import ml_dtypes
import concourse.bass as bass
import concourse.mybir as mybir
from concourse.bass_utils import run_bass_kernel_spmd

BF = ml_dtypes.bfloat16



ENGINES = ("PE", "ACT", "DVE", "POOL", "SP")
DMAQ = ("SP", "ACT", "POOL")
NLANE = 8


class Sched:
    def __init__(self, nc):
        self.nc = nc
        self.prog = {e: [] for e in ENGINES}
        self.cnt = {e: 0 for e in ENGINES}
        self.dmacnt = {q: 0 for q in DMAQ}
        self.cccnt = 0
        self.maxw = {e: {} for e in ENGINES}
        self.tokview = {}
        self.state = {}
        self.esem = {e: nc.alloc_semaphore(f"sem_{e}") for e in ENGINES}
        self.dsem = {(q, l): nc.alloc_semaphore(f"dma_{q}{l}")
                     for q in DMAQ for l in range(NLANE)}
        self.dlane = {q: 0 for q in DMAQ}
        self.dcnt = {(q, l): 0 for q in DMAQ for l in range(NLANE)}
        self.csem = nc.alloc_semaphore("sem_cc")
        self.nsteps = 0

    def _semkey(self, tok):
        if tok[0] == "eng":
            return "eng:" + tok[1]
        if tok[0] == "dma":
            return "dma:%s%d" % tok[1]
        return "cc:"

    def _sem(self, tok):
        if tok[0] == "eng":
            return self.esem[tok[1]]
        if tok[0] == "dma":
            return self.dsem[tok[1]]
        return self.csem

    def _tokval(self, tok):
        return tok[-1]

    def _merge(self, eng, snap):
        mw = self.maxw[eng]
        for k, v in snap.items():
            if mw.get(k, 0) < v:
                mw[k] = v

    def _wait(self, eng, tok):
        if tok is None:
            return
        key = self._semkey(tok)
        v = self._tokval(tok)
        snap = self.tokview.get((key, v))
        if self.maxw[eng].get(key, 0) >= v:
            if snap:
                self._merge(eng, snap)
            return
        if snap:
            self._merge(eng, snap)
        self.maxw[eng][key] = v
        sem = self._sem(tok)
        self.prog[eng].append(("wait", sem, v))

    def _deps_for(self, reads, writes):
        deps = []
        for rk in reads:
            st = self.state.get(rk)
            if st and st["w"] is not None:
                deps.append(st["w"])
        for wk in writes:
            st = self.state.get(wk)
            if st:
                if st["w"] is not None:
                    deps.append(st["w"])
                deps.extend(st["r"])
        return deps

    def _update_state(self, reads, writes, tok):
        for wk in writes:
            self.state[wk] = {"w": tok, "r": []}
        for rk in reads:
            st = self.state.setdefault(rk, {"w": None, "r": []})
            st["r"].append(tok)

    def step(self, eng, emit, reads=(), writes=(), extra_deps=(), n_inc=1):
        assert eng in ENGINES
        for d in self._deps_for(list(reads), list(writes)) + list(extra_deps):
            self._wait(eng, d)
        self.cnt[eng] += n_inc
        tok = ("eng", eng, self.cnt[eng])
        snap = dict(self.maxw[eng])
        snap["eng:" + eng] = self.cnt[eng]
        self.tokview[("eng:" + eng, self.cnt[eng])] = snap
        self.prog[eng].append(("emit", emit, self.esem[eng], n_inc))
        self._update_state(reads, writes, tok)
        self.nsteps += 1
        return tok

    def dma(self, q, pairs, reads=(), writes=(), extra_deps=()):
        assert q in DMAQ
        lane = self.dlane[q]
        self.dlane[q] = (lane + 1) % NLANE
        prev = self.dcnt[(q, lane)]
        if prev:
            self._wait(q, ("dma", (q, lane), prev))
        for d in self._deps_for(list(reads), list(writes)) + list(extra_deps):
            self._wait(q, d)
        n = len(pairs)
        self.dcnt[(q, lane)] += 16 * n
        tok = ("dma", (q, lane), self.dcnt[(q, lane)])
        self.tokview[(self._semkey(tok), self.dcnt[(q, lane)])] = dict(self.maxw[q])
        self.prog[q].append(("dma", list(pairs), self.dsem[(q, lane)]))
        self._update_state(reads, writes, tok)
        self.nsteps += 1
        return tok

    def cc(self, emit, reads=(), writes=(), extra_deps=()):
        if self.cccnt:
            self._wait("POOL", ("cc", self.cccnt))
        for d in self._deps_for(list(reads), list(writes)) + list(extra_deps):
            self._wait("POOL", d)
        self.cccnt += 1
        tok = ("cc", self.cccnt)
        self.tokview[("cc:", self.cccnt)] = dict(self.maxw["POOL"])
        self.prog["POOL"].append(("emit", emit, self.csem, 1))
        self._update_state(reads, writes, tok)
        self.nsteps += 1
        return tok

    def alias(self, new_keys, old_keys, drop_old=True):
        toks = []
        for ok in old_keys:
            st = self.state.get(ok)
            if st:
                if st["w"] is not None:
                    toks.append(st["w"])
                toks.extend(st["r"])
                if drop_old:
                    del self.state[ok]
        for nk in new_keys:
            st = self.state.setdefault(nk, {"w": None, "r": []})
            st["r"] = st["r"] + toks

    def barrier_wait(self, eng, toks):
        for t in toks:
            self._wait(eng, t)

    def build(self):
        nc = self.nc
        with nc.Block() as blk:
            decs = {"SP": blk.sync, "PE": blk.tensor, "DVE": blk.vector,
                    "ACT": blk.scalar, "POOL": blk.gpsimd}
            for eng in ENGINES:
                items = self.prog[eng]

                def body(e, items=items):
                    for it in items:
                        if it[0] == "wait":
                            e.wait_ge(it[1], it[2])
                        elif it[0] == "emit":
                            inst = it[1](e)
                            assert inst is not None, "emit must return last instruction"
                            inst.then_inc(it[2], it[3])
                        elif it[0] == "dma":
                            for (out, in_, kw) in it[1]:
                                e.dma_start(out=out, in_=in_, **kw).then_inc(it[2], 16)
                decs[eng](body)




F32 = mybir.dt.float32
BF16 = mybir.dt.bfloat16
AF = mybir.ActivationFunctionType
ALU = mybir.AluOpType
AX = mybir.AxisListType

D = 2048
NB = 8
TOK = NB * 128
EPS = 1e-5
ISQ = 0.08838834764831845
RG = [[0, 1], [2, 3], [4, 5], [6, 7]]
MISC_PREFS = ("N1", "Xn7", "oj", "KH", "VH", "gam", "cs", "kr32", "HE", "HT",
              "rold", "mk", "mu", "mc", "mv", "bzt", "zero", "hsc", "hsc2",
              "agc", "ags", "chc", "chox", "hint", "hin", "rowf", "xn2", "dum",
              "xnr")
QTA_PREFS = ("QH", "prb", "prt", "stg", "ostg", "kvdT", "qdT", "SG", "cT", "Xn")


def ts(i, n=128):
    return slice(i * n, (i + 1) * n)


class MK:
    def __init__(self, nc, stop_after="x4"):
        self.nc = nc
        self.s = Sched(nc)
        self.stop = stop_after
        self.uid = 0
        self.rot3 = 0
        self.decl_io()
        self.alloc()

    def u(self):
        self.uid += 1
        return self.uid

    def decl_io(self):
        nc = self.nc
        self.inp = {}

        def di(name, shape, dt=F32):
            t = nc.dram_tensor(name, list(shape), dt, kind="ExternalInput")
            self.inp[name] = t
            return t

        di("x0", (TOK, D))
        di("aWq", (D, D)); di("aWk", (D, D)); di("aWv", (D, D)); di("aWo", (D, D))
        di("abq", (128, 16)); di("abk", (128, 16))
        di("abv", (1, D)); di("abo", (1, D))
        di("aln_g", (1, D)); di("aln_b", (1, D))
        di("fln1", (1, D)); di("fln2", (1, D))
        di("fkvd", (D, 512)); di("fqd", (D, 512))
        di("fku", (512, 1024)); di("fvu", (512, D)); di("fqu", (512, 1024))
        di("frk", (D, 1024)); di("frq", (512, 1024))
        di("fow", (D, D))
        di("fshg", (D, 1024)); di("fshu", (D, 1024)); di("fshd", (1024, D))
        di("frg", (7, D, 1024)); di("fru", (7, D, 1024)); di("frd", (7, 1024, D))
        di("frw", (D, 7)); di("frb", (1, 7))
        di("bWzT", (D, D)); di("bWnT", (D, D))
        di("bbz", (1, D)); di("bbn", (1, D))
        di("ident", (128, 128), BF16)
        di("mask1", (128, 128), BF16)
        di("mask2", (128, 128), BF16)
        di("ropeC", (TOK, 1024))
        di("ropeS", (TOK, 1024))
        di("par", (128, 1))
        di("identf", (128, 128))

        self.out = nc.dram_tensor("y", [TOK, D], F32, kind="ExternalOutput")

        nd = nc.dram_tensor
        self.agin_K = nd("agin_K", [2048, 1024], BF16, kind="Internal")
        self.agout_K = nd("agout_K", [4096, 1024], BF16, kind="Internal")
        self.agin_V = nd("agin_V", [1024, 2048], BF16, kind="Internal")
        self.agout_V = nd("agout_V", [2048, 2048], BF16, kind="Internal")
        self.agin_S = nd("agin_S", [128, 128], F32, kind="Internal")
        self.agout_S = nd("agout_S", [256, 128], F32, kind="Internal")
        self.qtd = nd("qtd", [2048, 1024], BF16, kind="Internal")
        self.otd = nd("otd", [2048, 1024], BF16, kind="Internal")
        self.rsp = nd("rsp", [TOK, D], F32, kind="Internal")

    def alloc(self):
        A = self.nc.alloc_sbuf_tensor
        self.STREAM = A("STREAM", [128, NB * D], F32)
        self.XT = A("XT", [128, 16 * 1024], BF16)
        self.QTA = A("QTA", [128, 16 * 1024], BF16)
        self.WA = A("WA", [128, 8 * 1024], BF16)
        self.WB = A("WB", [128, 8 * 1024], BF16)
        self.MISC = A("MISC", [128, 16 * 1024], BF16)
        self.PRB = self.QTA[:, 2048:8192]
        self.XNS = self.QTA[:, 2048:6144]
        self.PRT = self.QTA[:, 8192:12288]
        self.STG = self.QTA[:, 12288:14336]
        self.OSTG = self.QTA[:, 14336:16384]
        self.SCRa = self.MISC[:, 0:4096].bitcast(F32)
        self.SCRb = self.MISC[:, 4096:6144]
        self.SCRb2 = self.MISC[:, 14336:16384]
        self.KH = self.MISC[:, 0:4096]
        self.VH = self.MISC[:, 4096:8192]
        self.GAM = self.MISC[:, 8192:12288]
        self.CS = self.MISC[:, 12288:16384].bitcast(F32)
        self.MV = A("MV", [128, 4 * 24 + 12], F32)
        self.SM1 = A("SM1", [128, 8], F32)
        self.NM = A("NM", [128, 6], F32)
        self.SUMS = A("SUMS", [128, 6], F32)
        self.RCP = A("RCP", [128, 3], F32)
        self.DG = A("DG", [128, 3 * 128], BF16)
        self.IDENT = A("IDENT", [128, 128], BF16)
        self.MK1 = A("MK1", [128, 128], BF16)
        self.MK2 = A("MK2", [128, 128], BF16)
        self.BIASC = A("BIASC", [128, 32], F32)
        self.BROWB = A("BROWB", [1, D + 16], BF16)
        self.ROWB = A("ROWB", [1, D], BF16)
        self.PAR = A("PAR", [128, 2], F32)
        self.RT = A("RT", [128, 64], F32)
        self.WS = A("WS", [128, 64], F32)
        self.FRW = A("FRW", [128, 112], BF16)
        self.IDF = A("IDF", [128, 128], F32)
        self.ROWF = self.MISC.bitcast(F32)[0:1, 3968:4992]
        self.ONE1F = A("ONE1F", [1, 128], F32)
        self.ONE1 = A("ONE1", [1, 128], BF16)
        P = self.nc.alloc_psum_tensor
        self.PS = [P("PSA", [128, 1024], F32), P("PSB", [128, 1024], F32),
                   P("PSC", [128, 1024], F32), P("PSD", [128, 1024], F32)]

    def st(self, eng, emit, r=(), w=(), x=()):
        return self.s.step(eng, emit, reads=r, writes=w, extra_deps=x)

    def heat(self):
        def emit(e):
            return e.matmul(self.PS[2][:, 768:896], self.ONE1[:], self.ONE1[:],
                            start=True, stop=True)
        self.st("PE", emit, r=(("one1", 0),), w=(("heat", 0), ("ps", 2, 1)))

    def dma(self, q, pairs, r=(), w=(), x=()):
        return self.s.dma(q, [(a, b, k) for (a, b, k) in pairs], reads=r, writes=w,
                          extra_deps=x)

    def load_whalf(self, wsrc_ap, rows, c0, cw, slot, wkey, x=()):
        nch = rows // 128
        dst = slot[:, 0:nch * cw].rearrange("p (c k) -> p c k", k=cw)
        src = wsrc_ap[:, c0:c0 + cw].rearrange("(c p) k -> p c k", p=128)
        return self.dma("POOL", [(dst, src, {})], w=(wkey,), x=x)

    def phase_init(self):
        self.dma("SP", [(self.STREAM[:, j * D:(j + 1) * D],
                         self.inp["x0"].ap()[ts(j), :], {}) for j in range(NB)],
                 w=[("R", j) for j in range(NB)])
        self.dma("SP", [(self.IDENT[:], self.inp["ident"].ap(), {}),
                        (self.MK1[:], self.inp["mask1"].ap(), {}),
                        (self.MK2[:], self.inp["mask2"].ap(), {}),
                        (self.PAR[:, 0:1], self.inp["par"].ap(), {})],
                 w=(("consts", 0),))
        self.dma("SP", [(self.BIASC[:, 0:16], self.inp["abq"].ap(), {}),
                        (self.BIASC[:, 16:32], self.inp["abk"].ap(), {})],
                 w=(("biasc", 0),))
        self.dma("POOL", [(self.BROWB[0:1, 0:D], self.inp["abv"].ap(), {}),
                          (self.BROWB[0:1, D:D + 7], self.inp["frb"].ap(), {})],
                 w=(("browb", 0),))
        self.st("POOL", lambda e: e.memset(self.ONE1[:], 1.0), w=(("one1", 0),))
        self.st("POOL", lambda e: e.memset(self.ONE1F[:], 1.0), w=(("one1", 1),))
        self.dma("SP", [(self.IDF[:], self.inp["identf"].ap(), {})], w=(("idf", 0),))
        self.dma("POOL", [(self.FRW[:].rearrange("p (c k) -> p c k", k=7),
                           self.inp["frw"].ap().rearrange("(c p) k -> p c k", p=128), {})],
                 w=(("frw", 0),))
        self.st("DVE", lambda e: e.tensor_scalar(self.PAR[:, 1:2], self.PAR[:, 0:1],
                                                 -1.0, 1.0, ALU.mult, ALU.add),
                r=(("consts", 0),), w=(("consts", 1),))

    def load_gamma(self, na, nb=None):
        for i, name in enumerate([na] + ([nb] if nb else [])):
            self.dma("POOL", [(self.ROWB[:], self.inp[name].ap(), {})],
                     w=(("rowb", 0),))
            for hf in range(2):
                ps = self.PS[3]
                def emit(e, i=i, hf=hf, ps=ps):
                    last = None
                    for q in range(2):
                        last = e.matmul(ps[:, q * 512:(q + 1) * 512], self.ONE1[:],
                                        self.ROWB[0:1, hf * 1024 + q * 512:
                                                  hf * 1024 + (q + 1) * 512],
                                        start=True, stop=True)
                    return last
                self.st("PE", emit, r=(("rowb", 0), ("one1", 0)),
                        w=(("ps", 3, 0), ("ps", 3, 1)))
                self.st("DVE", lambda e, i=i, hf=hf, ps=ps: e.tensor_copy(
                    self.GAM[:, i * D + hf * 1024: i * D + (hf + 1) * 1024], ps[:]),
                    r=(("ps", 3, 0), ("ps", 3, 1)), w=(("gam", i),))

    def bn_mv(self, src_ap, rkeys, col, on_act=False, rms=False):
        MV = self.MV
        col = col % 4
        base = 24 * col
        mu = MV[:, 96 + 2 * col:97 + 2 * col]
        var = MV[:, 97 + 2 * col:98 + 2 * col]
        if on_act:
            dum = self.MISC[:, 4096 + (col % 2) * 2048: 4096 + (col % 2) * 2048 + 2048]
            dkey = ("dum", col % 2)
            if not rms:
                self.st("ACT", lambda e: e.activation(out=dum, in_=src_ap,
                                                      func=AF.Identity, accum_out=mu),
                        r=rkeys, w=(dkey, ("mv", col)))
            self.st("ACT", lambda e: e.activation(out=dum, in_=src_ap,
                                                  func=AF.Square, accum_out=var),
                    r=rkeys + (("mv", col),) if not rms else rkeys,
                    w=(dkey, ("mv", col)))
            if rms:
                return self.st("DVE", lambda e: e.tensor_scalar_mul(var, var, 1.0 / D),
                               r=(("mv", col),), w=(("mv", col),))
            def fin(e):
                e.tensor_scalar_mul(mu, mu, 1.0 / D)
                return e.tensor_scalar_mul(var, var, 1.0 / D)
            self.st("DVE", fin, r=(("mv", col),), w=(("mv", col),))
            ng = MV[:, 104 + col:105 + col]
            self.st("DVE", lambda e: e.tensor_scalar_mul(ng, mu, -1.0),
                    r=(("mv", col),), w=(("mv", col),))
            return self.st("DVE", lambda e: e.scalar_tensor_tensor(
                var, mu, ng, var, ALU.mult, ALU.add),
                r=(("mv", col),), w=(("mv", col),))
        def e1(e):
            for c in range(4):
                last = e.bn_stats(MV[:, base + 6 * c:base + 6 * (c + 1)],
                                  src_ap[:, 512 * c:512 * (c + 1)])
            return last
        self.st("DVE", e1, r=rkeys, w=(("mvscr", col),))
        return self.st("DVE", lambda e: e.bn_aggr(MV[:, 96 + 2 * col:98 + 2 * col],
                                                  MV[:, base:base + 24]),
                       r=(("mvscr", col),), w=(("mv", col),))

    def ln_scalars(self, col, rms=False):
        MV, SM1 = self.MV, self.SM1
        col = col % 4
        mu = MV[:, 96 + 2 * col:97 + 2 * col]
        var = MV[:, 97 + 2 * col:98 + 2 * col]
        nmu = SM1[:, 2 * col:2 * col + 1]
        rst = SM1[:, 2 * col + 1:2 * col + 2]
        if rms is True:
            def e1(e):
                e.memset(nmu, 0.0)
                return e.scalar_tensor_tensor(rst, mu, mu, var, ALU.mult, ALU.add)
        elif rms == "direct":
            def e1(e):
                e.memset(nmu, 0.0)
                return e.tensor_scalar(rst, var, EPS, None, ALU.add)
        else:
            def e1(e):
                e.tensor_scalar_mul(nmu, mu, -1.0)
                return e.tensor_scalar(rst, var, EPS, None, ALU.add)
        self.st("DVE", e1, r=(("mv", col),), w=(("sm1", col),))
        if rms is True:
            self.st("DVE", lambda e: e.tensor_scalar_add(rst, rst, EPS),
                    r=(("sm1", col),), w=(("sm1", col),))
        self.st("DVE", lambda e: e.reciprocal(rst, rst),
                r=(("sm1", col),), w=(("sm1", col),))
        return self.st("ACT", lambda e: e.activation(out=rst, in_=rst, func=AF.Sqrt),
                       r=(("sm1", col),), w=(("sm1", col),))

    def apply_ln(self, dst_ap, src_ap, col, rkeys, wkey):
        SM1 = self.SM1
        col = col % 4
        return self.st("DVE",
                       lambda e: e.tensor_scalar(dst_ap, src_ap,
                                                 SM1[:, 2 * col:2 * col + 1],
                                                 SM1[:, 2 * col + 1:2 * col + 2],
                                                 ALU.add, ALU.mult),
                       r=tuple(rkeys) + (("sm1", col),), w=(wkey,))

    def transpose_block(self, src_bf16, src_key, j, dst, dst_name, nch=16,
                        dst_colw=1024, dst_key=None):
        toks = []
        for half in range((nch + 7) // 8):
            ps = self.PS[3]
            pkey = [("ps", 3, 0), ("ps", 3, 1)]
            n = min(8, nch - half * 8)
            def emit_T(e, half=half, ps=ps, n=n):
                last = None
                for c in range(n):
                    cc = half * 8 + c
                    last = e.matmul(ps[:, c * 128:(c + 1) * 128],
                                    src_bf16[:, cc * 128:(cc + 1) * 128],
                                    self.IDENT[:], start=(c % 4 == 0),
                                    stop=(c % 4 == 3 or c == n - 1))
                return last
            self.st("PE", emit_T, r=(src_key, ("consts", 0)), w=pkey)
            def emit_ev(e, half=half, ps=ps, n=n, j=j):
                dv = dst[:, :].rearrange("p (c t) -> p c t", t=dst_colw)
                return e.tensor_copy(dv[:, half * 8:half * 8 + n, j * 128:(j + 1) * 128],
                                     ps[:, 0:n * 128].rearrange("p (c t) -> p c t", t=128))
            wk = dst_key if dst_key is not None else (dst_name, (half, j))
            wks = [wk] if not (isinstance(wk, tuple) and wk[0] == "stgp") else \
                  [("stg", 2 * wk[1]), ("stg", 2 * wk[1] + 1)]
            toks.append(self.st("DVE", emit_ev, r=pkey, w=wks))
        return toks

    def proj(self, wname, rows, cols, consume):
        W = self.inp[wname].ap()
        cw = min(cols, 512)
        nh = (cols + cw - 1) // cw
        for hidx in range(nh):
            slot = self.WA if hidx % 2 == 0 else self.WB
            wkey = ("Wslot", hidx % 2)
            wtok = self.load_whalf(W, rows, hidx * cw, min(cw, cols - hidx * cw),
                                   slot, wkey)
            consume(hidx, hidx * cw, min(cw, cols - hidx * cw), slot, wtok, wkey)

    def attn_a(self, feed_ffn=True):
        self.load_gamma("aln_g", "aln_b")
        for j in range(NB):
            self.norm_attn_a(j)
        self.kproj_fm("aWk", ("dram", self.agin_K, lambda o: [(ts(o), slice(0, 128))]),
                      bias_col0=16, half_keys=True)
        self.tK = self.s.cc(lambda e: e.collective_compute(
            "AllGather", ALU.bypass, replica_groups=RG,
            ins=[self.agin_K.ap().opt()], outs=[self.agout_K.ap().opt()]),
            reads=[("aginK", o) for o in range(16)], writes=(("agoutK", 0),))
        self.vproj_tm("aWv", self.agin_V, brow_bf16_off=0)
        self.tV = self.s.cc(lambda e: e.collective_compute(
            "AllGather", ALU.bypass, replica_groups=RG,
            ins=[self.agin_V.ap().opt()], outs=[self.agout_V.ap().opt()]),
            reads=[("aginV", j) for j in range(NB)], writes=(("agoutV", 0),))
        self.kproj_fm("aWq", ("dram", self.qtd, lambda o: [(ts(o), slice(0, 128))]),
                      bias_col0=0, dstkey="qtd", qscale=ISQ, half_keys=True)
        self.attn_core(scale=ISQ)
        if feed_ffn:
            self.s.alias([self.xkey(j) for j in range(NB)],
                         self.keys_pref(*QTA_PREFS, "stgp", "cs", "dum", "N1"))
            self.dma("POOL", [(self.BROWB[0:1, 0:D], self.inp["abo"].ap(), {})],
                     w=(("browb", 0),))
            self.load_gamma_pre_ffn()
            self.oproj("aWo", brow_bf16_off=0,
                       on_block_done=lambda j: self.normA_ln_rms(j, 0, spill=True))
            for j in range(NB):
                self.normB_to_xt(j)
        else:
            self.dma("POOL", [(self.BROWB[0:1, 0:D], self.inp["abo"].ap(), {})],
                     w=(("browb", 0),))
            self.oproj("aWo", brow_bf16_off=0)

    def norm_attn_a(self, j):
        R = self.STREAM[:, j * D:(j + 1) * D]
        p = j % 2
        N1 = self.MISC[:, p * 4096:(p + 1) * 4096].bitcast(F32)
        Xn = self.XNS[:, p * 2048:(p + 1) * 2048]
        rkey = ("R", j)
        self.bn_mv(R, (rkey,), j)
        self.heat()
        self.ln_scalars(j)
        self.apply_ln(N1[:], R, j, (rkey,), ("N1", p))
        self.bn_mv(N1[:], (("N1", p),), j)
        self.heat()
        self.ln_scalars(j)
        self.apply_ln(N1[:], N1[:], j, (("N1", p),), ("N1", p))
        self.st("DVE", lambda e: e.tensor_mul(N1[:], N1[:], self.GAM[:, 0:D]),
                r=(("N1", p), ("gam", 0)), w=(("N1", p),))
        self.st("DVE", lambda e: e.tensor_add(Xn[:], N1[:], self.GAM[:, D:2 * D]),
                r=(("N1", p), ("gam", 1)), w=(("Xn", p),))
        self.transpose_block(Xn[:], ("Xn", p), j, self.XT, "XT")

    def xt_keys(self):
        return [("XT", (h, j)) for h in range(2) for j in range(NB)]

    def kproj_fm(self, wname, dst, bias_col0=None, dstkey="aginK",
                 src_keys=None, src_arena=None, src_colw=1024, nd=16,
                 rows_in=None, cols=None, qscale=None, half_keys=False):
        src_arena = src_arena if src_arena is not None else self.XT
        src_keys = src_keys if src_keys is not None else self.xt_keys()
        rows_in = rows_in if rows_in is not None else D
        cols = cols if cols is not None else D
        def consume(hidx, c0, cw, slot, wtok, wkey):
            for ol in range(cw // 128):
                o = (c0 + ol * 128) // 128
                for h2 in range(2):
                    it = o * 2 + h2
                    ps = self.PS[it % 2]
                    pkey = ("ps", it % 2, 0)
                    def emit(e, ol=ol, h2=h2, ps=ps, slot=slot, cw=cw):
                        last = None
                        for d in range(nd):
                            last = e.matmul(
                                ps[:, 0:512],
                                slot[:, d * cw + ol * 128: d * cw + (ol + 1) * 128],
                                src_arena[:, d * src_colw + h2 * 512: d * src_colw + (h2 + 1) * 512],
                                start=(d == 0), stop=(d == nd - 1))
                        return last
                    if half_keys:
                        skeys = [("XT", (hf, j2)) for hf in range(2)
                                 for j2 in range(h2 * 4, h2 * 4 + 4)]
                    else:
                        skeys = list(src_keys)
                    self.st("PE", emit, r=[wkey] + skeys, w=(pkey,), x=(wtok,))
                    def mkev(tgt):
                        def ev(e, ps=ps, tgt=tgt, o=o):
                            if bias_col0 is not None and qscale is not None:
                                return e.tensor_scalar(tgt, ps[:, 0:512],
                                                       self.BIASC[:, bias_col0 + o:bias_col0 + o + 1],
                                                       qscale, ALU.add, ALU.mult)
                            if bias_col0 is not None:
                                return e.tensor_scalar_add(tgt, ps[:, 0:512],
                                                           self.BIASC[:, bias_col0 + o:bias_col0 + o + 1])
                            if qscale is not None:
                                return e.tensor_scalar_mul(tgt, ps[:, 0:512], qscale)
                            return e.tensor_copy(tgt, ps[:, 0:512])
                        return ev
                    if dst[0] == "sbuf":
                        tgt = dst[1](o, h2)
                        self.st("DVE", mkev(tgt), r=(pkey, ("biasc", 0)), w=((dstkey, o),))
                        continue
                    stg = self.STG[:, (it % 4) * 512: (it % 4) * 512 + 512]
                    self.st("DVE", mkev(stg), r=(pkey, ("biasc", 0)), w=(("stg", it % 4),))
                    dramdst, row_fn = dst[1], dst[2]
                    pairs = []
                    for (rsl, psl) in row_fn(o):
                        pairs.append((dramdst.ap()[rsl, h2 * 512:(h2 + 1) * 512],
                                      stg[psl, :], {}))
                    self.dma("SP", pairs, r=(("stg", it % 4),), w=((dstkey, o),))
        self.proj(wname, rows_in, cols, consume)

    def vproj_tm(self, wname, dramdst, brow_bf16_off=None,
                 src_keys=None, dstkey="aginV", cols=D,
                 src_arena=None, src_colw=1024, nd=16, rows_in=None):
        src_arena = src_arena if src_arena is not None else self.XT
        rows_in = rows_in if rows_in is not None else D
        def consume(hidx, c0, cw, slot, wtok, wkey):
            for j in range(NB):
                for r2 in range(cw // 512):
                    it = j * (cw // 512) + r2
                    ps = self.PS[it % 2]
                    pkey = ("ps", it % 2, 0)
                    def emit(e, j=j, r2=r2, ps=ps, slot=slot, cw=cw, c0=c0):
                        last = None
                        for d in range(nd):
                            last = e.matmul(
                                ps[:, 0:512],
                                src_arena[:, d * src_colw + j * 128: d * src_colw + (j + 1) * 128],
                                slot[:, d * cw + r2 * 512: d * cw + (r2 + 1) * 512],
                                start=(d == 0), stop=(brow_bf16_off is None and d == nd - 1))
                        if brow_bf16_off is None:
                            return last
                        return e.matmul(ps[:, 0:512], self.ONE1[:],
                                        self.BROWB[0:1, brow_bf16_off + c0 + r2 * 512:
                                                   brow_bf16_off + c0 + (r2 + 1) * 512],
                                        start=False, stop=True)
                    if src_keys is None:
                        skeys = [("XT", (hf, j)) for hf in range(2)]
                    else:
                        skeys = list(src_keys)
                    self.st("PE", emit, r=[wkey, ("one1", 0), ("browb", 0)] + skeys,
                            w=(pkey,), x=(wtok,))
                    stg = self.STG[:, (it % 4) * 512: (it % 4) * 512 + 512]
                    self.st("DVE", lambda e, ps=ps, stg=stg: e.tensor_copy(stg, ps[:, 0:512]),
                            r=(pkey,), w=(("stg", it % 4),))
                    self.dma("SP", [(dramdst.ap()[ts(j), c0 + r2 * 512: c0 + (r2 + 1) * 512],
                                     stg, {})],
                             r=(("stg", it % 4),), w=((dstkey, j),))
        self.proj(wname, rows_in, cols, consume)

    def load_head(self, h):
        hb = h % 2
        kcols = self.KH[:, hb * 2048:(hb + 1) * 2048]
        vcols = self.VH[:, hb * 2048:(hb + 1) * 2048]
        self.dma("SP", [(kcols[:, r * 1024:(r + 1) * 1024],
                         self.agout_K.ap()[r * 2048 + h * 128: r * 2048 + (h + 1) * 128, :], {})
                        for r in range(2)],
                 r=(("agoutK", 0),), w=(("KH", hb),), x=(self.tK,))
        self.dma("SP", [(vcols[:, r * 1024:(r + 1) * 1024].rearrange("p (s d) -> p s d", d=128),
                         self.agout_V.ap()[r * 1024:(r + 1) * 1024, h * 128:(h + 1) * 128]
                         .rearrange("(s p) d -> p s d", p=128), {})
                        for r in range(2)],
                 r=(("agoutV", 0),), w=(("VH", hb),), x=(self.tV,))
        qh = self.QTA[:, hb * 1024:(hb + 1) * 1024]
        self.dma("SP", [(qh, self.qtd.ap()[ts(h), :], {})],
                 r=[("qtd", o) for o in range(16)], w=(("QH", hb),))
        return kcols, vcols, qh

    def attn_core(self, scale):
        n = 16 * NB
        LAG = globals().get("ATTN_LAG", 2)
        ctx = {}
        hd = {}
        for i in range(n + LAG):
            if i < n:
                h, j = divmod(i, NB)
                if j == 0:
                    hd[h] = self.load_head(h)
                kcols, vcols, qh = hd[h]
                ctx[i] = self.attn_scores(h, j, h % 2, kcols, qh, i)
                self.attn_soft(ctx[i])
            if i >= LAG:
                self.attn_tav(ctx[i - LAG])
                del ctx[i - LAG]
        hd.clear()

    def attn_scores(self, h, j, hb, kcols, qh, it):
        s = self
        ke = (j + 1) * 128
        tot = 2 * ke
        nbuf = 1 if tot <= 1024 else 2
        bufs = []
        bidx = []
        for n in range(nbuf):
            i = [0, 1][self.rot3 % 2]
            bidx.append(i)
            bufs.append(self.PS[i])
            self.rot3 += 1
        pkeys = [("ps", i, hh) for i in bidx for hh in range(2)]

        mspans = [(j * 128, (j + 1) * 128), (ke + j * 128, ke + (j + 1) * 128)]
        plan = []
        for side in range(2):
            base, koff, pos = side * ke, side * 1024, 0
            while pos < ke:
                n = min(512, ke - pos, 512 - (base + pos) % 512)
                plan.append((base + pos, n, ("k", koff + pos)))
                pos += n
        for mi, (m0, _) in enumerate(mspans):
            plan.append((m0, 128, ("m", mi)))
        firsts, lasts = {}, {}
        for idx, (a0, n, _) in enumerate(plan):
            bk = a0 // 512
            firsts.setdefault(bk, idx)
            lasts[bk] = idx
        def emit_scores(e):
            last = None
            for idx, (a0, n, rd) in enumerate(plan):
                bk = a0 // 512
                b = bufs[a0 // 1024]
                dst = b[:, a0 % 1024:a0 % 1024 + n]
                if rd[0] == "k":
                    last = e.matmul(dst, qh[:, j * 128:(j + 1) * 128],
                                    kcols[:, rd[1]:rd[1] + n],
                                    start=(firsts[bk] == idx), stop=(lasts[bk] == idx))
                else:
                    mk = self.MK1 if rd[1] == 0 else self.MK2
                    last = e.matmul(dst, self.IDENT[:], mk[:],
                                    start=(firsts[bk] == idx), stop=(lasts[bk] == idx))
            return last
        s.st("PE", emit_scores,
             r=[("KH", hb), ("QH", hb), ("consts", 0)], w=pkeys)
        return dict(h=h, j=j, hb=hb, it=it, ke=ke, tot=tot, nbuf=nbuf,
                    bufs=bufs, bidx=bidx, pkeys=pkeys)

    def attn_soft(self, c):
        s = self
        h, j, hb, it = c["h"], c["j"], c["hb"], c["it"]
        tot, nbuf, bufs, pkeys = c["tot"], c["nbuf"], c["bufs"], c["pkeys"]
        p2 = it % 3
        nm = self.NM[:, p2 * 2: p2 * 2 + 2]
        def emit_max(e):
            last = None
            for n in range(nbuf):
                w = min(1024, tot - n * 1024)
                last = e.reduce_max(nm[:, n:n + 1], bufs[n][:, 0:w], axis=AX.X, negate=True)
            return last
        s.st("DVE", emit_max, r=pkeys, w=(("nm", p2),))
        if nbuf == 2:
            s.st("DVE", lambda e: e.tensor_tensor(nm[:, 0:1], nm[:, 0:1], nm[:, 1:2], ALU.min),
                 r=(("nm", p2),), w=(("nm", p2),))
        prb = self.PRB[:, p2 * 2048:(p2 + 1) * 2048]
        sm = self.SUMS[:, p2 * 2: p2 * 2 + 2]
        def emit_exp(e):
            last = None
            for n in range(nbuf):
                w = min(1024, tot - n * 1024)
                last = e.activation(out=prb[:, n * 1024:n * 1024 + w],
                                    in_=bufs[n][:, 0:w], func=AF.Exp,
                                    bias=nm[:, 0:1], scale=1.0,
                                    accum_out=sm[:, n:n + 1])
            return last
        s.st("ACT", emit_exp, r=pkeys + [("nm", p2)], w=(("prb", p2),))
        rc = self.RCP[:, p2:p2 + 1]
        dg = self.DG[:, p2 * 128:(p2 + 1) * 128]
        if nbuf == 2:
            s.st("DVE", lambda e: (e.tensor_add(rc, sm[:, 0:1], sm[:, 1:2])),
                 r=(("prb", p2),), w=(("rc", p2),))
            s.st("DVE", lambda e: e.reciprocal(rc, rc),
                 r=(("rc", p2),), w=(("rc", p2),))
        else:
            s.st("DVE", lambda e: e.reciprocal(rc, sm[:, 0:1]),
                 r=(("prb", p2),), w=(("rc", p2),))
        s.st("DVE", lambda e: e.tensor_scalar_mul(dg, self.IDENT[:], rc),
             r=(("rc", p2), ("consts", 0)), w=(("dg", p2),))

    def attn_tav(self, c):
        s = self
        h, j, hb, it = c["h"], c["j"], c["hb"], c["it"]
        tot, nbuf = c["tot"], c["nbuf"]
        p2 = it % 3
        pq = it % 2
        prb = self.PRB[:, p2 * 2048:(p2 + 1) * 2048]
        dg = self.DG[:, p2 * 128:(p2 + 1) * 128]
        vcols = self.VH[:, hb * 2048:(hb + 1) * 2048]
        prt = self.PRT[:, pq * 2048:(pq + 1) * 2048]
        nch = tot // 128
        nrnd = (nch + 3) // 4
        avi = 2 + it % 2
        av = self.PS[avi][:, 512:640]
        avkey = ("ps", avi, 1)
        def emit_avpart(rnd):
            n = min(4, nch - rnd * 4)
            def emit_av(e, rnd=rnd, n=n):
                last = None
                for c in range(rnd * 4, rnd * 4 + n):
                    if c <= j:
                        vt = vcols[:, c * 128:(c + 1) * 128]
                    else:
                        vt = vcols[:, 1024 + (c - (j + 1)) * 128: 1024 + (c - j) * 128]
                    last = e.matmul(av, vt, prt[:, c * 128:(c + 1) * 128],
                                    start=(c == 0), stop=(c == nch - 1))
                return last
            s.st("PE", emit_av, r=[("VH", hb), ("prt", (pq, rnd))], w=(avkey,))
        for rnd in range(nrnd):
            n = min(4, nch - rnd * 4)
            pti = 2 + rnd % 2
            pt = self.PS[pti]
            ptk = ("ps", pti, 0)
            def emit_T(e, rnd=rnd, n=n, pt=pt):
                last = None
                for c in range(n):
                    cc = rnd * 4 + c
                    last = e.matmul(pt[:, c * 128:(c + 1) * 128],
                                    prb[:, cc * 128:(cc + 1) * 128],
                                    dg, start=(c == 0), stop=(c == n - 1))
                return last
            s.st("PE", emit_T, r=(("prb", p2), ("dg", p2)), w=(ptk,))
            def emit_ev(e, rnd=rnd, n=n, pt=pt):
                dst = prt[:, rnd * 512:rnd * 512 + n * 128]
                if hasattr(e, "tensor_copy"):
                    return e.tensor_copy(dst, pt[:, 0:n * 128])
                return e.copy(dst, pt[:, 0:n * 128])
            s.st("DVE" if rnd % 2 == 0 else "ACT", emit_ev, r=(ptk,),
                 w=(("prt", (pq, rnd)),))
            if rnd >= 1:
                emit_avpart(rnd - 1)
        emit_avpart(nrnd - 1)
        ob = self.OSTG[:, hb * 1024: hb * 1024 + 1024]
        def emit_oev(e):
            return e.tensor_copy(ob[:, j * 128:(j + 1) * 128], av)
        s.st("DVE", emit_oev, r=(avkey,), w=(("ostg", (hb, j)),))
        if j == NB - 1:
            s.dma("SP", [(self.otd.ap()[ts(h), :], ob, {})],
                  r=[("ostg", (hb, jj)) for jj in range(NB)], w=(("otd", h),))

    def oproj(self, wname, brow_bf16_off, resid_arena=None, rname="R",
              on_block_done=None):
        resid_arena = resid_arena if resid_arena is not None else self.STREAM
        OJ = self.MISC
        def consume(hidx, c0, cw, slot, wtok, wkey):
            for j in range(NB):
                js = j % 2
                oj = OJ[:, js * 2048: js * 2048 + 2048]
                self.dma("SP", [(oj.rearrange("p (c k) -> p c k", k=128),
                                 self.otd.ap()[:, j * 128:(j + 1) * 128]
                                 .rearrange("(c p) k -> p c k", p=128), {})],
                         r=[("otd", hh) for hh in range(16)], w=(("oj", js),))
                for r2 in range(cw // 512):
                    it = j * (cw // 512) + r2
                    ps = self.PS[it % 2]
                    pkey = ("ps", it % 2, 0)
                    def emit(e, j=j, r2=r2, ps=ps, slot=slot, cw=cw, c0=c0, oj=oj):
                        last = None
                        for d in range(16):
                            last = e.matmul(
                                ps[:, 0:512],
                                oj[:, d * 128:(d + 1) * 128],
                                slot[:, d * cw + r2 * 512:d * cw + (r2 + 1) * 512],
                                start=(d == 0),
                                stop=(brow_bf16_off is None and d == 15))
                        if brow_bf16_off is None:
                            return last
                        return e.matmul(ps[:, 0:512], self.ONE1[:],
                                        self.BROWB[0:1, brow_bf16_off + c0 + r2 * 512:
                                                   brow_bf16_off + c0 + (r2 + 1) * 512],
                                        start=False, stop=True)
                    self.st("PE", emit, r=[wkey, ("one1", 0), ("browb", 0), ("oj", js)],
                            w=(pkey,), x=(wtok,))
                    dst = resid_arena[:, j * D + c0 + r2 * 512: j * D + c0 + (r2 + 1) * 512]
                    def ev(e, ps=ps, dst=dst):
                        return e.tensor_add(dst, dst, ps[:, 0:512])
                    self.st("DVE", ev, r=(pkey, (rname, j)), w=((rname, j),))
                    if on_block_done is not None and hidx == 3 and r2 == cw // 512 - 1:
                        on_block_done(j)
        self.proj(wname, D, D, consume)


    def keys_pref(self, *prefixes):
        return [k for k in list(self.s.state) if isinstance(k, tuple) and k[0] in prefixes]

    def xkey(self, j):
        return ("Xn", j) if j < 7 else ("Xn7", 0)

    def xslot(self, j):
        if j < 7:
            return self.QTA[:, 2048 + j * 2048: 2048 + (j + 1) * 2048]
        return self.MISC[:, 12288:14336]

    def normA_ln_rms(self, j, gam_idx, spill=False, ln=True, rms=True):
        R = self.STREAM[:, j * D:(j + 1) * D]
        rkey = ("R", j)
        xn = self.xslot(j)
        if spill:
            self.dma("SP", [(self.rsp.ap()[ts(j), :], R, {})], r=(rkey,),
                     w=(("rsp", j),))
        if ln:
            self.bn_mv(R, (rkey,), j, on_act=True)
            self.heat()
            self.ln_scalars(j)
            if not rms:
                self.apply_ln(xn, R, j, (rkey,), self.xkey(j))
                return
            self.apply_ln(R, R, j, (rkey,), rkey)
        self.bn_mv(R, (rkey,), j, on_act=True, rms=True)
        self.heat()
        self.ln_scalars(j, rms="direct")
        SM1 = self.SM1
        jm = j % 4
        self.st("DVE", lambda e: e.scalar_tensor_tensor(
            xn, R, SM1[:, 2 * jm + 1:2 * jm + 2],
            self.GAM[:, gam_idx * D:(gam_idx + 1) * D], ALU.mult, ALU.mult),
            r=(rkey, ("sm1", j % 4), ("gam", gam_idx)), w=(self.xkey(j),))

    def normB_to_xt(self, j):
        self.transpose_block(self.xslot(j), self.xkey(j), j, self.XT, "XT")

    def rope_proj(self, wname, dst_dram, dstkey_base, src_arena, src_colw, nd,
                  src_keys, tag):
        s = self
        kr32 = self.SCRa.bitcast(F32) if False else self.SCRa
        W = self.inp[wname].ap()
        wtoks, slots = [], []
        for hw in range(2):
            slot = self.WA if hw == 0 else self.WB
            wkey = ("Wslot", hw)
            wtoks.append(self.load_whalf(W, nd * 128, hw * 512, 512, slot, wkey))
            slots.append((slot, wkey))
        s.s.alias([("cs", tag)], s.keys_pref("cs", "Xn7"))
        for j in range(NB):
            cskey = ("cs", tag)
            self.dma("SP", [(self.CS[:, 0:1024], self.inp["ropeC"].ap()[ts(j), :], {}),
                            (self.CS[:, 1024:2048], self.inp["ropeS"].ap()[ts(j), :], {})],
                     w=(cskey,))
            ps = self.PS[0]
            pk = [("ps", 0, 0), ("ps", 0, 1)]
            def emit_mm(e, j=j, ps=ps):
                last = None
                for hw in range(2):
                    slot = slots[hw][0]
                    for d in range(nd):
                        last = e.matmul(ps[:, hw * 512:(hw + 1) * 512],
                                        src_arena[:, d * src_colw + j * 128:
                                                  d * src_colw + (j + 1) * 128],
                                        slot[:, d * 512:(d + 1) * 512],
                                        start=(d == 0), stop=(d == nd - 1))
                return last
            s.st("PE", emit_mm, r=[slots[0][1], slots[1][1]] + list(src_keys),
                 w=pk, x=wtoks)
            s.st("DVE", lambda e, ps=ps: e.tensor_copy(self.SCRa[:, 0:1024], ps[:]),
                 r=pk, w=(("kr32", 0),))
            x3 = self.SCRa[:, 0:1024].rearrange("p (h i) -> p h i", i=64)
            t3 = self.SCRa[:, 1024:2048].rearrange("p (h i) -> p h i", i=64)
            c3 = self.CS[:, 0:1024].rearrange("p (h i) -> p h i", i=64)
            s3 = self.CS[:, 1024:2048].rearrange("p (h i) -> p h i", i=64)
            s.st("POOL", lambda e: e.tensor_mul(t3[:, :, 0:32], x3[:, :, 32:64], s3[:, :, 0:32]),
                 r=(("kr32", 0), cskey), w=(("krT", 0),))
            s.st("POOL", lambda e: e.tensor_mul(t3[:, :, 32:64], x3[:, :, 0:32], s3[:, :, 32:64]),
                 r=(("kr32", 0), cskey), w=(("krT", 1),))
            s.st("POOL", lambda e: e.tensor_mul(self.SCRa[:, 0:1024], self.SCRa[:, 0:1024],
                                                self.CS[:, 0:1024]),
                 r=(("kr32", 0), cskey, ("krT", 0), ("krT", 1)), w=(("kr32", 0),))
            s.st("POOL", lambda e: e.tensor_add(self.SCRb[:, 0:1024], self.SCRa[:, 0:1024],
                                                self.SCRa[:, 1024:2048]),
                 r=(("kr32", 0), ("krT", 0), ("krT", 1)), w=(("xnr", 0),))
            stg = self.STG[:, (j % 2) * 1024:(j % 2) * 1024 + 1024]
            self.transpose_block(self.SCRb[:, 0:1024], ("xnr", 0), 0, stg,
                                 "stg", nch=8, dst_colw=128,
                                 dst_key=("stgp", j % 2))
            pairs = []
            for c in range(8):
                pairs.append((dst_dram.ap()[(2 * c) * 128 + 64:(2 * c) * 128 + 128, ts(j)],
                              stg[0:64, c * 128:(c + 1) * 128], {}))
                pairs.append((dst_dram.ap()[(2 * c + 1) * 128 + 64:(2 * c + 1) * 128 + 128, ts(j)],
                              stg[64:128, c * 128:(c + 1) * 128], {}))
            self.dma("SP", pairs, r=[("stg", 2 * (j % 2)), ("stg", 2 * (j % 2) + 1)],
                     w=((dstkey_base, 8 + j),))

    def load_gamma_pre_ffn(self):
        s = self.s
        s.alias([("gam", 0), ("gam", 1), ("cs", 0), ("kr32", 0)],
                self.keys_pref("KH", "VH", "HE", "HT", "oj", "rold", "mk", "mu",
                               "mc", "mv", "bzt", "zero", "hsc", "hsc2", "agc",
                               "ags", "chc", "chox", "hint", "hin", "rowf",
                               "xn2", "N1", "cs", "gam", "dum", "Xn7", "xnr"))
        self.load_gamma("fln1", "fln2")

    def ffn(self, tag, pre_normed=False):
        s = self.s
        if not pre_normed:
            self.load_gamma_pre_ffn()
            s.alias([self.xkey(j) for j in range(NB)],
                    self.keys_pref(*QTA_PREFS, *MISC_PREFS, "stgp"))
            for j in range(NB):
                self.normA_ln_rms(j, 0, spill=True)
            for j in range(NB):
                self.normB_to_xt(j)
        s.alias([("kvdT", c) for c in range(4)] + [("qdT", c) for c in range(4)]
                + [("stg", 0), ("stg", 1), ("ostg", (0, 0))],
                self.keys_pref(*QTA_PREFS))
        kvcol = lambda o, h2: self.QTA[:, o * 1024 + h2 * 512: o * 1024 + (h2 + 1) * 512]
        qdcol = lambda o, h2: self.QTA[:, 4096 + o * 1024 + h2 * 512: 4096 + o * 1024 + (h2 + 1) * 512]
        self.kproj_fm("fkvd", ("sbuf", kvcol), dstkey="kvdT", cols=512,
                      half_keys=True)
        self.kproj_fm("fqd", ("sbuf", qdcol), dstkey="qdT", cols=512, qscale=ISQ,
                      half_keys=True)
        kvdT = self.QTA[:, 0:4096]
        qdT = self.QTA[:, 4096:8192]
        kvkeys = [("kvdT", c) for c in range(4)]
        qdkeys = [("qdT", c) for c in range(4)]
        self.rope_proj("frk", self.agin_K, "aginK", self.XT, 1024, 16,
                       self.xt_keys(), tag)
        def rows_kp(o):
            return [(slice((2 * o) * 128, (2 * o) * 128 + 64), slice(0, 64)),
                    (slice((2 * o + 1) * 128, (2 * o + 1) * 128 + 64), slice(64, 128))]
        self.kproj_fm("fku", ("dram", self.agin_K, rows_kp), dstkey="aginK",
                      src_keys=kvkeys, src_arena=kvdT, src_colw=1024, nd=4,
                      rows_in=512, cols=1024)
        self.tK = s.cc(lambda e: e.collective_compute(
            "AllGather", ALU.bypass, replica_groups=RG,
            ins=[self.agin_K.ap().opt()], outs=[self.agout_K.ap().opt()]),
            reads=[("aginK", o) for o in range(16)], writes=(("agoutK", 0),))
        self.vproj_tm("fvu", self.agin_V, src_keys=kvkeys, src_arena=kvdT,
                      src_colw=1024, nd=4, rows_in=512)
        self.tV = s.cc(lambda e: e.collective_compute(
            "AllGather", ALU.bypass, replica_groups=RG,
            ins=[self.agin_V.ap().opt()], outs=[self.agout_V.ap().opt()]),
            reads=[("aginV", j) for j in range(NB)], writes=(("agoutV", 0),))
        self.kproj_fm("fqu", ("dram", self.qtd, rows_kp), dstkey="qtd",
                      src_keys=qdkeys, src_arena=qdT, src_colw=1024, nd=4,
                      rows_in=512, cols=1024)
        self.rope_proj("frq", self.qtd, "qtd", qdT, 1024, 4, qdkeys, tag + 1)
        s.alias([("KH", 0), ("KH", 1), ("VH", 0), ("VH", 1)],
                self.keys_pref("N1", "kr32", "dum"))
        s.alias([("QH", 0), ("QH", 1), ("prb", 0), ("prb", 1)]
                + [("prt", (p, r)) for p in range(2) for r in range(4)]
                + [("ostg", (h, j)) for h in range(2) for j in range(NB)],
                self.keys_pref("kvdT", "qdT"))
        self.attn_core(scale=ISQ)
        s.alias([("oj", 0), ("oj", 1)], self.keys_pref("KH"))
        s.alias([self.xkey(j) for j in range(NB)],
                self.keys_pref(*QTA_PREFS, "stgp", "cs", "dum", "N1"))
        self.oproj("fow", brow_bf16_off=None,
                   on_block_done=lambda j: self.normA_ln_rms(j, 1, ln=False))
        for j in range(NB):
            self.normB_to_xt(j)
        self.router(tag)
        self.moe(tag)
        s.alias([("rold", 0)], self.keys_pref("HE", "HT"))
        for j in range(NB):
            self.dma("SP", [(self.SCRa[:, 0:D], self.rsp.ap()[ts(j), :], {})],
                     r=(("rsp", j),), w=(("rold", 0),))
            self.st("DVE", lambda e, j=j: e.tensor_add(
                self.STREAM[:, j * D:(j + 1) * D],
                self.STREAM[:, j * D:(j + 1) * D], self.SCRa[:, 0:D]),
                r=(("rold", 0), ("R", j)), w=(("R", j),))

    def router(self, tag):
        s = self
        RT, WS, ONES7 = self.RT, self.WS, self.RT[:, 56:63]
        self.st("DVE", lambda e: e.memset(ONES7, 1.0), w=(("ones7", 0),))
        for j in range(NB):
            ps = self.PS[0]
            pk = ("ps", 0, 0)
            def emit(e, j=j, ps=ps):
                for d in range(16):
                    e.matmul(ps[:, 0:7],
                             self.XT[:, d * 1024 + j * 128: d * 1024 + (j + 1) * 128],
                             self.FRW[:, d * 7:(d + 1) * 7],
                             start=(d == 0), stop=False)
                return e.matmul(ps[:, 0:7], self.ONE1[:],
                                self.BROWB[0:1, D:D + 7],
                                start=False, stop=True)
            s.st("PE", emit, r=[("frw", 0), ("one1", 0), ("browb", 0)] + self.xt_keys(),
                 w=(pk,))
            p = RT[:, 0:7]
            s.st("ACT", lambda e, ps=ps, p=p: e.activation(out=p, in_=ps[:, 0:7],
                                                           func=AF.Sigmoid),
                 r=(pk,), w=(("rt", 0),))
            m1 = RT[:, 7:8]; eq = RT[:, 8:15]; cum = RT[:, 15:22]
            m1f = RT[:, 22:29]; pp2 = RT[:, 29:36]; m2 = RT[:, 36:37]
            eq2 = RT[:, 37:44]; cum2 = RT[:, 44:51]; f2 = RT[:, 51:52]
            t1 = RT[:, 52:53]; den = RT[:, 53:54]; rec = RT[:, 54:55]
            D_ = [("rt", 0)]
            def dve(fn, rk, wk):
                return s.st("DVE", fn, r=rk, w=wk)
            dve(lambda e: e.reduce_max(m1, p, axis=AX.X), D_, [("rt", 1)])
            dve(lambda e: e.tensor_single_scalar(eq, p, m1, ALU.is_equal),
                D_ + [("rt", 1)], [("rt", 2)])
            dve(lambda e: e.tensor_tensor_scan(cum, ONES7, eq, 0.0, ALU.mult, ALU.add),
                [("rt", 2), ("ones7", 0)], [("rt", 3)])
            dve(lambda e: e.tensor_single_scalar(cum, cum, 1.0, ALU.is_equal),
                [("rt", 3)], [("rt", 3)])
            dve(lambda e: e.tensor_mul(m1f, eq, cum), [("rt", 2), ("rt", 3)], [("rt", 4)])
            dve(lambda e: e.scalar_tensor_tensor(pp2, m1f, -1e9, p, ALU.mult, ALU.add),
                [("rt", 4)] + D_, [("rt", 5)])
            dve(lambda e: e.reduce_max(m2, pp2, axis=AX.X), [("rt", 5)], [("rt", 6)])
            dve(lambda e: e.tensor_single_scalar(eq2, pp2, m2, ALU.is_equal),
                [("rt", 5), ("rt", 6)], [("rt", 7)])
            dve(lambda e: e.tensor_tensor_scan(cum2, ONES7, eq2, 0.0, ALU.mult, ALU.add),
                [("rt", 7), ("ones7", 0)], [("rt", 8)])
            dve(lambda e: e.tensor_single_scalar(cum2, cum2, 1.0, ALU.is_equal),
                [("rt", 8)], [("rt", 8)])
            dve(lambda e: e.tensor_mul(eq2, eq2, cum2), [("rt", 7), ("rt", 8)], [("rt", 9)])
            dve(lambda e: e.tensor_add(m1f, m1f, eq2), [("rt", 4), ("rt", 9)], [("rt", 10)])
            dve(lambda e: e.tensor_add(den, m1, m2), [("rt", 1), ("rt", 6)], [("rt", 11)])
            dve(lambda e: e.reciprocal(rec, den), [("rt", 11)], [("rt", 12)])
            dve(lambda e: e.tensor_scalar_mul(rec, rec, 0.125), [("rt", 12)], [("rt", 12)])
            dve(lambda e: e.tensor_mul(eq, p, m1f), D_ + [("rt", 10)], [("rt", 13)])
            dve(lambda e, j=j: e.tensor_scalar_mul(self.WS[:, j * 8 + 1:j * 8 + 8],
                                                   eq, rec),
                [("rt", 13), ("rt", 12)], [("ws", j)])
            dve(lambda e, j=j: e.memset(self.WS[:, j * 8:j * 8 + 1], 0.125),
                [], [("ws", j)])

    def moe(self, tag, moe_cb=None):
        s = self.s
        HE = self.MISC[:, 0:8192]
        HT = self.MISC[:, 8192:16384]
        s.alias([("HE", j) for j in range(NB)] + [("HT", j) for j in range(NB)],
                self.keys_pref(*MISC_PREFS))
        s.alias([("SG", 0), ("SG", 1)], self.keys_pref(*QTA_PREFS))
        for e_ in range(8):
            wg = self.inp["fshg"].ap() if e_ == 0 else self.inp["frg"].ap()[e_ - 1]
            wu = self.inp["fshu"].ap() if e_ == 0 else self.inp["fru"].ap()[e_ - 1]
            wd = self.inp["fshd"].ap() if e_ == 0 else self.inp["frd"].ap()[e_ - 1]
            for hd in range(2):
                tg = self.load_whalf(wg, D, hd * 512, 512, self.WA, ("Wslot", 0))
                tu = self.load_whalf(wu, D, hd * 512, 512, self.WB, ("Wslot", 1))
                for j in range(NB):
                    ps = self.PS[j % 2]
                    pklo = ("ps", j % 2, 0)
                    pkhi = ("ps", j % 2, 1)
                    def emit_g(e, j=j, ps=ps):
                        last = None
                        for d in range(16):
                            last = e.matmul(ps[:, 0:512],
                                            self.XT[:, d * 1024 + j * 128: d * 1024 + (j + 1) * 128],
                                            self.WA[:, d * 512:(d + 1) * 512],
                                            start=(d == 0), stop=(d == 15))
                        return last
                    s.step("PE", emit_g, reads=[("Wslot", 0)] + self.xt_keys(),
                           writes=(pklo,), extra_deps=(tg,))
                    sg = self.QTA[:, (j % 2) * 512:(j % 2) * 512 + 512]
                    self.st("ACT", lambda e, ps=ps, sg=sg: e.activation(
                        out=sg, in_=ps[:, 0:512], func=AF.Silu),
                        r=(pklo,), w=(("SG", j % 2),))
                    def emit_u(e, j=j, ps=ps):
                        last = None
                        for d in range(16):
                            last = e.matmul(ps[:, 512:1024],
                                            self.XT[:, d * 1024 + j * 128: d * 1024 + (j + 1) * 128],
                                            self.WB[:, d * 512:(d + 1) * 512],
                                            start=(d == 0), stop=(d == 15))
                        return last
                    s.step("PE", emit_u, reads=[("Wslot", 1)] + self.xt_keys(),
                           writes=(pkhi,), extra_deps=(tu,))
                    he = HE[:, j * 1024 + hd * 512: j * 1024 + (hd + 1) * 512]
                    self.st("DVE", lambda e, j=j, ps=ps, sg=sg, he=he, e_=e_:
                            e.scalar_tensor_tensor(he, sg, self.WS[:, j * 8 + e_:j * 8 + e_ + 1],
                                                   ps[:, 512:1024], ALU.mult, ALU.mult),
                            r=(("SG", j % 2), pkhi, ("ws", j)), w=(("HE", j),))
            for ho in range(2):
                slot = self.WA if ho == 0 else self.WB
                td = self.load_whalf(wd, 1024, ho * 1024, 1024, slot, ("Wslot", ho))
                def emit_down(j, slot=slot, ho=ho, td=td, e_=e_):
                    def emit_d(e, j=j, slot=slot):
                        last = None
                        for r2 in range(2):
                            for c in range(8):
                                last = e.matmul(self.PS[3][:, r2 * 512:(r2 + 1) * 512],
                                                HT[:, j * 1024 + c * 128: j * 1024 + (c + 1) * 128],
                                                slot[:, c * 1024 + r2 * 512: c * 1024 + (r2 + 1) * 512],
                                                start=(c == 0), stop=(c == 7))
                        return last
                    self.st("PE", emit_d, r=(("HT", j), ("Wslot", ho)),
                            w=(("ps", 3, 0), ("ps", 3, 1)), x=(td,))
                    def emit_a(e, j=j, ho=ho):
                        return e.tensor_add(
                            self.STREAM[:, j * D + ho * 1024: j * D + (ho + 1) * 1024],
                            self.STREAM[:, j * D + ho * 1024: j * D + (ho + 1) * 1024],
                            self.PS[3][:])
                    self.st("DVE", emit_a,
                            r=(("ps", 3, 0), ("ps", 3, 1), ("R", j)), w=(("R", j),))
                    if moe_cb is not None and e_ == 7 and ho == 1:
                        moe_cb(j)
                for j in range(NB + (2 if ho == 0 else 0)):
                    if ho == 0 and j >= 2:
                        emit_down(j - 2)
                    if ho == 0 and j < NB:
                        def emit_T(e, j=j):
                            last = None
                            for c in range(8):
                                last = e.matmul(self.PS[2][:, c * 128:(c + 1) * 128],
                                                HE[:, j * 1024 + c * 128: j * 1024 + (c + 1) * 128],
                                                self.IDENT[:], start=(c % 4 == 0),
                                                stop=(c % 4 == 3))
                            return last
                        self.st("PE", emit_T, r=(("HE", j), ("consts", 0)),
                                w=(("ps", 2, 0), ("ps", 2, 1)))
                        self.st("DVE", lambda e, j=j: e.tensor_copy(
                            HT[:, j * 1024:(j + 1) * 1024], self.PS[2][:]),
                            r=(("ps", 2, 0), ("ps", 2, 1)), w=(("HT", j),))
                    elif ho == 1:
                        emit_down(j)


    def scan_block(self):
        s = self.s
        M = self.MISC.bitcast(F32)
        Mk = [M[:, 0:512], M[:, 4992:5504]]
        Mu = [M[:, 512:1024], M[:, 5504:6016]]
        Mc = [M[:, 1024:1536], M[:, 6016:6528]]
        Mv = [M[:, 1536:2048], M[:, 6528:7040]]
        BZT, BNT = M[:, 2048:2560], M[:, 2560:3072]
        HSC = M[:, 3072:3584]
        HSCB = M[:, 7168:7680]
        ZERO = M[:, 3584:3712]
        AGC, AGS = M[:, 3712:3776], M[:, 3776:3840]
        CHC, CHS = M[:, 3840:3856], M[:, 3856:3872]
        CHOX = M[:, 3872:3889]
        HINT = M[:, 3889:3897]
        HIN = M[:, 3897:3905]
        QF = self.QTA.bitcast(F32)
        mkeys = ([("mk", 0), ("mu", 0), ("mc", 0), ("mv", 0), ("bzt", 0),
                  ("zero", 0), ("agc", 0), ("ags", 0), ("chc", 0), ("chox", 0),
                  ("hint", 0), ("hin", 0), ("rowf", 0), ("xn2", 0)]
                 + [("hsc", c) for c in range(4)]
                 + [("hsc2", (c, j)) for c in range(4) for j in range(NB)])
        s.alias(mkeys, self.keys_pref(*MISC_PREFS))
        s.alias([("cT", (c, j)) for c in range(4) for j in range(NB)],
                self.keys_pref(*QTA_PREFS))
        self.st("POOL", lambda e: e.memset(ZERO, 0.0), w=(("zero", 0),))
        for j in range(NB):
            R = self.STREAM[:, j * D:(j + 1) * D]
            self.bn_mv(R, (("R", j),), 3)
            self.ln_scalars(3)
            self.apply_ln(self.SCRb2[:], R, 3, (("R", j),), ("xn2", 0))
            self.transpose_block(self.SCRb2[:], ("xn2", 0), j, self.XT, "XT")
        self.s.alias([("hscB", c) for c in range(4)],
                     self.keys_pref("xn2", "Xn", "Xn7"))
        for q in range(4):
            self.scan_quarter(q, M, Mk, Mu, Mc, Mv, BZT, BNT, HSC, HSCB, ZERO,
                              AGC, AGS, CHC, CHS, CHOX, HINT, HIN, QF)

    def scan_quarter(self, q, M, Mk, Mu, Mc, Mv, BZT, BNT, HSC, HSCB, ZERO,
                     AGC, AGS, CHC, CHS, CHOX, HINT, HIN, QF):
        s = self.s
        qc = q * 512
        tz = self.load_whalf(self.inp["bWzT"].ap(), D, qc, 512, self.WA, ("Wslot", 0))
        tn = self.load_whalf(self.inp["bWnT"].ap(), D, qc, 512, self.WB, ("Wslot", 1))
        self.dma("SP", [(self.ROWF[0:1, 0:512], self.inp["bbz"].ap()[0:1, qc:qc + 512], {}),
                        (self.ROWF[0:1, 512:1024], self.inp["bbn"].ap()[0:1, qc:qc + 512], {})],
                 w=(("rowf", 0),))
        def emit_bz(e):
            e.matmul(self.PS[2][:, 0:512], self.ONE1F[:], self.ROWF[0:1, 0:512],
                     start=True, stop=True)
            return e.matmul(self.PS[2][:, 512:1024], self.ONE1F[:],
                            self.ROWF[0:1, 512:1024], start=True, stop=True)
        self.st("PE", emit_bz, r=(("rowf", 0), ("one1", 1)),
                w=(("ps", 2, 0), ("ps", 2, 1)))
        self.st("DVE", lambda e: (e.tensor_copy(BZT, self.PS[2][:, 0:512]),
                                  e.tensor_copy(BNT, self.PS[2][:, 512:1024]))[-1],
                r=(("ps", 2, 0), ("ps", 2, 1), ("mz", 0)), w=(("bzt", 0),))
        def p1_mm_el(jj):
            p = jj % 2
            mk_, mu_, mc_, mv_ = Mk[p], Mu[p], Mc[p], Mv[p]
            ps = self.PS[p]
            pklo, pkhi = ("ps", p, 0), ("ps", p, 1)
            def emit_k(e, ps=ps, jj=jj):
                last = None
                for d in range(16):
                    last = e.matmul(ps[:, 0:512],
                                    self.XT[:, d * 1024 + jj * 128: d * 1024 + (jj + 1) * 128],
                                    self.WA[:, d * 512:(d + 1) * 512],
                                    start=(d == 0), stop=(d == 15))
                return last
            def emit_u(e, ps=ps, jj=jj):
                last = None
                for d in range(16):
                    last = e.matmul(ps[:, 512:1024],
                                    self.XT[:, d * 1024 + jj * 128: d * 1024 + (jj + 1) * 128],
                                    self.WB[:, d * 512:(d + 1) * 512],
                                    start=(d == 0), stop=(d == 15))
                return last
            s.step("PE", emit_k, reads=[("Wslot", 0)] + self.xt_keys(), writes=(pklo,),
                   extra_deps=(tz,))
            s.step("PE", emit_u, reads=[("Wslot", 1)] + self.xt_keys(), writes=(pkhi,),
                   extra_deps=(tn,))
            self.st("DVE", lambda e: e.tensor_add(mk_, ps[:, 0:512], BZT),
                    r=(pklo, ("bzt", 0)), w=(("mk", p),))
            self.st("DVE", lambda e: e.tensor_add(mu_, ps[:, 512:1024], BNT),
                    r=(pkhi, ("bzt", 0)), w=(("mu", p),))
            self.st("ACT", lambda e: e.activation(out=mc_, in_=mk_, func=AF.Sigmoid,
                                                  scale=-1.0),
                    r=(("mk", p),), w=(("mc", p),))
            self.st("ACT", lambda e: e.activation(out=mv_, in_=mu_, func=AF.Sigmoid),
                    r=(("mu", p),), w=(("mv", p),))
            self.st("DVE", lambda e: e.tensor_scalar_add(mk_, mu_, 0.5),
                    r=(("mu", p), ("mk", p)), w=(("mk", p),))
            self.st("DVE", lambda e: e.tensor_max(mv_, mv_, mk_),
                    r=(("mv", p), ("mk", p)), w=(("mv", p),))
            self.st("DVE", lambda e: e.tensor_scalar(mk_, mc_, -1.0, 1.0, ALU.mult, ALU.add),
                    r=(("mc", p), ("mk", p)), w=(("mk", p),))
            self.st("DVE", lambda e: e.tensor_mul(mv_, mv_, mk_),
                    r=(("mv", p), ("mk", p)), w=(("mv", p),))

        def p1_T(jj):
            p = jj % 2
            mc_, mv_ = Mc[p], Mv[p]
            def emit_Tc(e, jj=jj):
                last = None
                for c in range(4):
                    last = e.matmul(self.PS[2][:, c * 128:(c + 1) * 128],
                                    mc_[:, c * 128:(c + 1) * 128], self.IDF[:],
                                    start=(c == 0), stop=(c == 3))
                for c in range(4):
                    last = e.matmul(self.PS[2][:, 512 + c * 128:512 + (c + 1) * 128],
                                    mv_[:, c * 128:(c + 1) * 128], self.IDF[:],
                                    start=(c == 0), stop=(c == 3))
                return last
            self.st("PE", emit_Tc, r=(("mc", p), ("mv", p), ("idf", 0)),
                    w=(("ps", 2, 0), ("ps", 2, 1)))
            def emit_ev(e, jj=jj):
                e.tensor_copy(QF[:, :4096].rearrange("p (cc j k) -> p cc j k", j=NB, k=128)[:, :, jj, :],
                              self.PS[2][:, 0:512].rearrange("p (cc k) -> p cc k", k=128))
                return e.tensor_copy(QF[:, 4096:].rearrange("p (cc j k) -> p cc j k", j=NB, k=128)[:, :, jj, :],
                                     self.PS[2][:, 512:1024].rearrange("p (cc k) -> p cc k", k=128))
            self.st("DVE", emit_ev, r=(("ps", 2, 0), ("ps", 2, 1)),
                    w=[("cT", (cc2, jj)) for cc2 in range(4)])
            for cc in range(4):
                ct = QF[:, (cc * NB + jj) * 128:(cc * NB + jj + 1) * 128]
                vt = QF[:, 4096 + (cc * NB + jj) * 128:4096 + (cc * NB + jj + 1) * 128]
                self.st("DVE", lambda e, ct=ct, vt=vt, cc=cc:
                        e.tensor_tensor_scan(HSC[:, cc * 128:(cc + 1) * 128], ct, vt,
                                             0.0, ALU.mult, ALU.add),
                        r=(("cT", (cc, jj)), ("zero", 0)), w=(("hsc", cc),))
                self.st("DVE", lambda e, cc=cc, jj=jj:
                        e.tensor_copy(AGS[:, cc * NB + jj:cc * NB + jj + 1],
                                      HSC[:, (cc + 1) * 128 - 1:(cc + 1) * 128]),
                        r=(("hsc", cc),), w=(("ags", 0),))
                self.st("DVE", lambda e, ct=ct, cc=cc:
                        e.tensor_tensor_scan(HSC[:, cc * 128:(cc + 1) * 128], ct,
                                             ZERO, 1.0, ALU.mult, ALU.add),
                        r=(("cT", (cc, jj)), ("zero", 0), ("hsc", cc)), w=(("hsc", cc),))
                self.st("DVE", lambda e, cc=cc, jj=jj:
                        e.tensor_copy(AGC[:, cc * NB + jj:cc * NB + jj + 1],
                                      HSC[:, (cc + 1) * 128 - 1:(cc + 1) * 128]),
                        r=(("hsc", cc),), w=(("agc", 0),))

        for jj in range(NB + 1):
            if jj < NB:
                p1_mm_el(jj)
            if jj >= 1:
                p1_T(jj - 1)
        self.dma("SP", [(self.agin_S.ap()[:, 0:64], AGC, {}),
                        (self.agin_S.ap()[:, 64:128], AGS, {})],
                 r=(("agc", 0), ("ags", 0)), w=(("aginS", 0),))
        tS = s.cc(lambda e: e.collective_compute(
            "AllGather", ALU.bypass, replica_groups=RG,
            ins=[self.agin_S.ap().opt()], outs=[self.agout_S.ap().opt()]),
            reads=(("aginS", 0),), writes=(("agoutS", 0),))
        for cc in range(4):
            nck = {"allow_slow_non_contiguous": True}
            self.dma("SP", [(CHC[:, 0:16:2], self.agout_S.ap()[0:128, cc * 8:(cc + 1) * 8], nck),
                            (CHC[:, 1:16:2], self.agout_S.ap()[128:256, cc * 8:(cc + 1) * 8], nck),
                            (CHS[:, 0:16:2], self.agout_S.ap()[0:128, 64 + cc * 8:64 + (cc + 1) * 8], nck),
                            (CHS[:, 1:16:2], self.agout_S.ap()[128:256, 64 + cc * 8:64 + (cc + 1) * 8], nck)],
                 r=(("agoutS", 0),), w=(("chc", 0),), x=(tS,))
            self.st("DVE", lambda e: e.memset(CHOX[:, 0:1], 0.5),
                    r=(("chc", 0),), w=(("chox", 0),))
            self.st("DVE", lambda e: e.tensor_tensor_scan(CHOX[:, 1:17], CHC, CHS,
                                                          0.5, ALU.mult, ALU.add),
                    r=(("chc", 0), ("chox", 0)), w=(("chox", 0),))
            self.st("DVE", lambda e: e.tensor_single_scalar(HINT, CHOX[:, 0:16:2],
                                                            self.PAR[:, 1:2], ALU.mult),
                    r=(("chox", 0), ("consts", 1)), w=(("hint", 0),))
            self.st("DVE", lambda e: e.scalar_tensor_tensor(HIN, CHOX[:, 1:17:2],
                                                            self.PAR[:, 0:1], HINT,
                                                            ALU.mult, ALU.add),
                    r=(("chox", 0), ("hint", 0), ("consts", 0)), w=(("hin", 0),))
            def p2_scan(jj, cc=cc):
                p = jj % 2
                hb = HSC if p == 0 else HSCB
                hk = ("hsc", cc) if p == 0 else ("hscB", cc)
                ct = QF[:, (cc * NB + jj) * 128:(cc * NB + jj + 1) * 128]
                vt = QF[:, 4096 + (cc * NB + jj) * 128:4096 + (cc * NB + jj + 1) * 128]
                self.st("DVE", lambda e, ct=ct, vt=vt:
                        e.tensor_tensor_scan(hb[:, cc * 128:(cc + 1) * 128], ct, vt,
                                             HIN[:, jj:jj + 1], ALU.mult, ALU.add),
                        r=(("cT", (cc, jj)), ("hin", 0), hk), w=(("hsc2", (cc, jj)), hk))
            def p2_tadd(jj, cc=cc):
                p = jj % 2
                hb = HSC if p == 0 else HSCB
                def emit_Th(e, cc=cc, jj=jj):
                    return e.matmul(self.PS[3][:, cc * 128:(cc + 1) * 128],
                                    hb[:, cc * 128:(cc + 1) * 128], self.IDF[:],
                                    start=True, stop=True)
                self.st("PE", emit_Th, r=(("hsc2", (cc, jj)), ("idf", 0)),
                        w=(("ps", 3, 0),))
                self.st("DVE", lambda e, cc=cc, jj=jj: e.tensor_add(
                    self.STREAM[:, jj * D + q * 512 + cc * 128: jj * D + q * 512 + (cc + 1) * 128],
                    self.STREAM[:, jj * D + q * 512 + cc * 128: jj * D + q * 512 + (cc + 1) * 128],
                    self.PS[3][:, cc * 128:(cc + 1) * 128]),
                    r=(("ps", 3, 0), ("R", jj)), w=(("R", jj),))
            for jj in range(NB + 1):
                if jj < NB:
                    p2_scan(jj)
                if jj >= 1:
                    p2_tadd(jj - 1)

    def phase_out(self):
        toks = []
        for j in range(NB):
            toks.append(self.dma("SP", [(self.out.ap()[ts(j), :],
                                         self.STREAM[:, j * D:(j + 1) * D], {})],
                                 r=(("R", j),), w=(("out", j),)))
        self.s.barrier_wait("SP", toks)

    def build(self):
        self.phase_init()
        import os
        fh = os.environ.get("MK_FEED", "1") == "1"
        self.attn_a(feed_ffn=fh and self.stop in ("x2", "x3", "x4"))
        if self.stop in ("x2", "x3", "x4"):
            self.ffn(tag=10, pre_normed=fh)
        if self.stop in ("x3", "x4"):
            self.scan_block()
        if self.stop == "x4":
            self.ffn(tag=20)
        self.phase_out()
        self.s.build()
        return self.nc


NAMEMAP = {
    "aWq": "L0_attn_a_Wq", "aWk": "L0_attn_a_Wk", "aWv": "L0_attn_a_Wv",
    "aWo": "L0_attn_a_Wo",
    "aln_g": "L0_attn_a_ln_g", "aln_b": "L0_attn_a_ln_b",
    "fln1": "L0_ffn_ln1_w", "fln2": "L0_ffn_ln2_w",
    "fkvd": "L0_ffn_kv_d_w", "fqd": "L0_ffn_q_d_w",
    "fku": "L0_ffn_k_u_w", "fvu": "L0_ffn_v_u_w", "fqu": "L0_ffn_q_u_w",
    "frk": "L0_ffn_rope_k_w", "frq": "L0_ffn_rope_q_w", "fow": "L0_ffn_o_w",
    "fshg": "L0_ffn_sh_gate", "fshu": "L0_ffn_sh_up", "fshd": "L0_ffn_sh_down",
    "frg": "L0_ffn_r_gate", "fru": "L0_ffn_r_up", "frd": "L0_ffn_r_down",
    "frw": "L0_ffn_router_w",
}


def make_inmaps(inputs, n_cores=8):
    X = np.asarray(inputs["L0_attn_a_X"], np.float32)
    shared = {}
    for k, ref in NAMEMAP.items():
        shared[k] = np.ascontiguousarray(np.asarray(inputs[ref], np.float32))
    shared["abq"] = np.ascontiguousarray(np.asarray(inputs["L0_attn_a_bq"], np.float32).reshape(16, 128).T)
    shared["abk"] = np.ascontiguousarray(np.asarray(inputs["L0_attn_a_bk"], np.float32).reshape(16, 128).T)
    shared["abv"] = np.asarray(inputs["L0_attn_a_bv"], np.float32).reshape(1, D)
    shared["abo"] = np.asarray(inputs["L0_attn_a_bo"], np.float32).reshape(1, D)
    shared["aln_g"] = shared["aln_g"].reshape(1, D)
    shared["aln_b"] = shared["aln_b"].reshape(1, D)
    shared["fln1"] = shared["fln1"].reshape(1, D)
    shared["fln2"] = shared["fln2"].reshape(1, D)
    shared["frb"] = np.asarray(inputs["L0_ffn_router_b"], np.float32).reshape(1, 7)
    shared["bWzT"] = np.ascontiguousarray(
        np.asarray(inputs["L0_attn_b_Wz"], np.float32).T)
    shared["bWnT"] = np.ascontiguousarray(
        np.asarray(inputs["L0_attn_b_Wn"], np.float32).T)
    shared["bbz"] = np.asarray(inputs["L0_attn_b_bz"], np.float32).reshape(1, D)
    shared["bbn"] = np.asarray(inputs["L0_attn_b_bn"], np.float32).reshape(1, D)
    shared["ident"] = np.eye(128, dtype=BF)
    shared["identf"] = np.eye(128, dtype=np.float32)
    ar = np.arange(128)
    diag = np.where(ar[None, :] > ar[:, None], np.float32(-1e30), np.float32(0)).astype(BF)
    allneg = np.full((128, 128), -1e30, np.float32).astype(BF)
    zeros = np.zeros((128, 128), BF)

    in_maps = []
    for c in range(n_cores):
        b, r = c // 2, c % 2
        m = dict(shared)
        blocks = [2 * j + r for j in range(NB)]
        m["x0"] = np.ascontiguousarray(
            X[b].reshape(16, 128, D)[blocks].reshape(TOK, D))
        m["mask1"] = diag if r == 0 else zeros
        m["mask2"] = allneg if r == 0 else diag
        pos = (np.asarray(blocks)[:, None] * 128 + ar[None, :]).reshape(TOK)
        inv_freq = 1.0 / 10000.0 ** (np.arange(0, 64, 2, np.float32) / 64)
        ang = np.outer(pos.astype(np.float32), inv_freq)
        cos = np.cos(np.concatenate([ang, ang], 1))
        sin = np.sin(ang)
        sa = np.concatenate([-sin, sin], 1)
        m["ropeC"] = np.ascontiguousarray(np.tile(cos, (1, 16)).astype(np.float32))
        m["ropeS"] = np.ascontiguousarray(np.tile(sa, (1, 16)).astype(np.float32))
        m["par"] = np.full((128, 1), float(r), np.float32)
        in_maps.append(m)
    return in_maps


def reassemble(res, n_cores=8):
    Y = np.zeros((n_cores // 2, 2048, D), np.float32)
    for c in range(n_cores):
        b, r = c // 2, c % 2
        y = np.asarray(res[c]["y"]).reshape(NB, 128, D)
        for j in range(NB):
            Y[b, (2 * j + r) * 128:(2 * j + r + 1) * 128] = y[j]
    return Y




_CACHE = {}


def kernel(**inputs):
    n_cores = 8
    if "nc" not in _CACHE:
        nc = bass.Bass("TRN2", target_bir_lowering=False)
        m = MK(nc, stop_after="x4")
        m.build()
        _CACHE["nc"] = nc
    in_maps = make_inmaps(inputs, n_cores)
    res = run_bass_kernel_spmd(_CACHE["nc"], in_maps, core_ids=list(range(n_cores)))
    return reassemble(res.results, n_cores)
